# version 10
# baseline (speedup 1.0000x reference)
import numpy as np
import concourse.bass as bass
import concourse.mybir as mybir
from concourse.bass_utils import run_bass_kernel_spmd
from concourse.alu_op_type import AluOpType as ALU
from contextlib import ExitStack

F32 = mybir.dt.float32
BF16 = mybir.dt.bfloat16
AF = mybir.ActivationFunctionType
AX = mybir.AxisListType.X

D = 2048
T = 2048
NT = 16
OWN = 1024
SCALE1 = float(1.0 / np.sqrt(128.0))
EPS = 1e-5
DEBUG = False
PHASES = 99


class _StopBuild(Exception):
    pass


class _FakeInst:
    __slots__ = ("incs",)

    def __init__(self):
        self.incs = []

    def then_inc(self, sem, n):
        self.incs.append((sem, n))
        return self


class _EngRec:

    def __init__(self):
        self.ops = []

    def __getattr__(self, name):
        def w(*a, **k):
            fi = _FakeInst()
            self.ops.append((name, a, k, fi))
            return fi
        return w


def build_nc():
    nc = bass.Bass("TRN2", target_bir_lowering=False, debug=False)
    x0_d = nc.dram_tensor("x0", [T, D], F32, kind="ExternalInput").ap()
    ccs_d = nc.dram_tensor("c_cs", [T, 128], F32, kind="ExternalInput").ap()
    cid_d = nc.dram_tensor("c_id", [128, 128], F32, kind="ExternalInput").ap()
    cidb_d = nc.dram_tensor("c_idb", [128, 128], BF16, kind="ExternalInput").ap()
    cones_d = nc.dram_tensor("c_ones", [128, 128], F32, kind="ExternalInput").ap()

    def din(name, shape):
        return nc.dram_tensor(name, shape, F32, kind="ExternalInput").ap()

    Wd = {
        "l0_wq": din("l0_wq", [D, 2048]), "l0_wk": din("l0_wk", [D, 512]),
        "l0_wv": din("l0_wv", [D, 512]), "l0_wphi": din("l0_wphi", [D, 2048]),
        "l0_wo": din("l0_wo", [D, 2048]), "wnorm": din("wnorm", [1, D]),
        "wgv": din("wgv", [16384, D]), "wout": din("wout", [D, 8192]),
        "l1_wq": din("l1_wq", [D, 2048]), "l1_bq": din("l1_bq", [1, 2048]),
        "l1_wk": din("l1_wk", [D, 256]), "l1_bk": din("l1_bk", [1, 256]),
        "l1_wv": din("l1_wv", [D, 256]), "l1_bv": din("l1_bv", [1, 256]),
        "l1_wo": din("l1_wo", [D, 2048]), "l1_bo": din("l1_bo", [1, 2048]),
        "wg": din("wg", [D, 8]), "bg": din("bg", [1, 8]),
        "wgate": din("wgate", [6, D, 1024]), "bgate": din("bgate", [6, 1024]),
        "wup": din("wup", [6, D, 1024]), "bup": din("bup", [6, 1024]),
        "wdown": din("wdown", [6, 1024, D]), "bdown": din("bdown", [6, D]),
        "ws_up": din("ws_up", [D, 2048]), "bs_up": din("bs_up", [1, 2048]),
        "ws_down": din("ws_down", [2048, D]), "bs_down": din("bs_down", [1, D]),
    }
    out_d = nc.dram_tensor("out", [OWN, D], F32, kind="ExternalOutput").ap()

    s_x1 = nc.dram_tensor("s_x1", [T, D], F32, kind="Internal").ap()
    s_qk = nc.dram_tensor("s_qk", [T, 2048], F32, kind="Internal").ap()
    s_kk = nc.dram_tensor("s_kk", [T, 512], F32, kind="Internal").ap()
    s_v = nc.dram_tensor("s_v", [T, 512], F32, kind="Internal").ap()
    s_u = nc.dram_tensor("s_u", [T, 8192], F32, kind="Internal").ap()
    s_wo = nc.dram_tensor("s_wo", [4, 64, 128, 512], BF16, kind="Internal").ap()
    s_att = nc.dram_tensor("s_att", [OWN, 2048], F32, kind="Internal").ap()
    s_y = nc.dram_tensor("s_y", [OWN, D], F32, kind="Internal").ap()

    ost = ExitStack(); ost.__enter__()
    ds = ost.enter_context(nc.semaphore("ds"))
    dg = ost.enter_context(nc.semaphore("dg"))
    ct = ost.enter_context(nc.semaphore("ct"))
    cv = ost.enter_context(nc.semaphore("cv"))
    cs = ost.enter_context(nc.semaphore("cs"))
    cg = ost.enter_context(nc.semaphore("cg"))
    cnt = {"ds": 0, "dg": 0, "ct": 0, "cv": 0, "cs": 0, "cg": 0}

    blk = ost.enter_context(nc.Block())
    eng = {}

    @blk.tensor
    def _(e): eng["t"] = e

    @blk.vector
    def _(e): eng["v"] = e

    @blk.scalar
    def _(e): eng["s"] = e

    @blk.gpsimd
    def _(e): eng["g"] = e

    @blk.sync
    def _(e): eng["y"] = e

    rtn, rve, rsc, rgp, rsy = eng["t"], eng["v"], eng["s"], eng["g"], eng["y"]
    tn, ve, sc, gp, sy = _EngRec(), _EngRec(), _EngRec(), _EngRec(), _EngRec()

    def TN(inst):
        inst.then_inc(ct, 1); cnt["ct"] += 1; return inst

    def TNq(inst):
        return inst

    def VE(inst):
        inst.then_inc(cv, 1); cnt["cv"] += 1; return inst

    def SC(inst):
        inst.then_inc(cs, 1); cnt["cs"] += 1; return inst

    def dmas(out, in_, slow=False):
        sy.dma_start(out=out, in_=in_, allow_slow_non_contiguous=slow).then_inc(ds, 16)
        cnt["ds"] += 16
        return cnt["ds"]

    def dmag(out, in_):
        gp.dma_start(out=out, in_=in_).then_inc(dg, 16)
        cnt["dg"] += 16
        return cnt["dg"]

    def dmas_split(buf, src, rows, ncol, parts=4):
        ns_ = []
        step = rows // 128 // parts
        for q in range(parts):
            ns_.append(dmas(
                buf[:, q * step * ncol:(q + 1) * step * ncol]
                .rearrange("p (n c) -> p n c", n=step),
                src[q * step * 128:(q + 1) * step * 128, :]
                .rearrange("(n p) c -> p n c", p=128)))
        return ns_

    def tn_wait_split(ns_):
        for n_ in ns_:
            tn.wait_ge(ds, n_); heatmm()

    def wt(e): e.wait_ge(ct, cnt["ct"])

    def wv(e): e.wait_ge(cv, cnt["cv"])

    def wsc(e): e.wait_ge(cs, cnt["cs"])

    def wds(e): e.wait_ge(ds, cnt["ds"])

    def wdg(e): e.wait_ge(dg, cnt["dg"])

    def fence():
        for e in (tn, ve, sc, gp, sy):
            wds(e); wdg(e); wt(e); wv(e); wsc(e)

    def sball(name, shape, dt=F32):
        return ost.enter_context(nc.sbuf_tensor(name, shape, dt))

    idt = sball("idt", [128, 128])
    idb = sball("idb", [128, 128], BF16)
    onest = sball("onest", [128, 128])
    mu0 = sball("mu0", [128, NT]); rs0 = sball("rs0", [128, NT])
    mu1 = sball("mu1", [128, NT]); rs1 = sball("rs1", [128, NT])
    epst = sball("epst", [128, 1])
    qgsum = sball("qgsum", [1, 2048])
    osbd = sball("osbd", [128, NT * 128])
    fgv = sball("fgv", [128, NT])
    fwo = sball("fwo", [128, NT])
    umax = sball("umax", [128, NT])
    mgv = sball("mgv", [128, 1]); thrgv = sball("thrgv", [128, 1])
    mwo = sball("mwo", [128, 1]); thrwo = sball("thrwo", [128, 1])
    wsel = sball("wsel", [128, 64])
    bqc = sball("bqc", [128, 16])
    bgB = sball("bgB", [128, 8])

    PS = [ost.enter_context(nc.psum_tensor(f"ps{i}", [128, 512], F32))
          for i in range(7)]
    PSB = ost.enter_context(nc.psum_tensor("psb", [128, 1024], BF16))

    def heatmm():
        TNq(tn.matmul(PS[5][0:1, 508:509], idt[:, 0:1], idt[:, 0:1],
                      start=False, stop=True))

    def heatv():
        tn.wait_ge(cv, cnt["cv"]); heatmm()

    def heats():
        tn.wait_ge(cs, cnt["cs"]); heatmm()

    dmas(idt[:], cid_d[:, :])
    dmas(idb[:], cidb_d[:, :])
    dmas(onest[:], cones_d[:, :])
    dmas(bqc[:], Wd["l1_bq"][0, :].rearrange("(m p) -> p m", p=128), slow=True)
    dmag(bgB[:], Wd["bg"][0:1, :].partition_broadcast(128))
    wds(ve); wdg(ve)
    VE(ve.memset(epst[:], EPS))
    VE(ve.memset(qgsum[:], 0.0))
    VE(ve.memset(umax[:], 0.0))
    VE(ve.memset(osbd[:], 0.0))

    def ln_stats(xt, mu_col, rs_col, scr, sq):
        wv(ve)
        VE(ve.reduce_sum(scr[:, 0:1], xt, axis=AX))
        heatv()
        wsc(sc); wv(sc)
        SC(sc.activation(sq, xt, AF.Square, accum_out=scr[:, 1:2]))
        heats()
        wv(ve); wsc(ve)
        VE(ve.tensor_scalar(mu_col, scr[:, 0:1], 1.0 / D, None, ALU.mult))
        VE(ve.tensor_scalar(scr[:, 2:3], scr[:, 1:2], 1.0 / D, None, ALU.mult))
        wv(ve)
        VE(ve.tensor_tensor(scr[:, 3:4], mu_col, mu_col, ALU.mult))
        wv(ve)
        VE(ve.tensor_sub(scr[:, 2:3], scr[:, 2:3], scr[:, 3:4]))
        wv(sc)
        SC(sc.activation(scr[:, 4:5], scr[:, 2:3], AF.Sqrt, bias=epst[:, 0:1]))
        wsc(ve)
        VE(ve.reciprocal(rs_col, scr[:, 4:5]))
        wv(ve)

    def apply_ln(dst, xt, mu_col, rs_col, scr):
        wsc(ve)
        VE(ve.tensor_scalar(scr, xt, mu_col, None, ALU.subtract))
        wv(ve); heatv()
        VE(ve.tensor_scalar(dst, scr, rs_col, None, ALU.mult))
        wv(ve); heatv()

    def build_fm(slab, fm, xtile, ntile, scr, mu, rs, src_d):
        for tl in range(4):
            t = slab * 4 + tl
            wv(sy); wsc(sy); wt(sy)
            n = dmas(xtile[:], src_d[t * 128:(t + 1) * 128, :])
            ve.wait_ge(ds, n)
            apply_ln(ntile[:], xtile[:], mu[:, t:t + 1], rs[:, t:t + 1], scr[:])
            for k in range(NT):
                wv(tn)
                TN(tn.transpose(PS[5][:, 0:128], ntile[:, k * 128:(k + 1) * 128], idt[:]))
                wt(ve)
                VE(ve.tensor_copy(fm[:, k * 512 + tl * 128: k * 512 + tl * 128 + 128],
                                  PS[5][:, 0:128]))
        wv(tn)

    def binary_round(dst, y, z, r, tb):
        wsc(ve)
        VE(ve.tensor_scalar(z, y, 128.5, None, ALU.add))
        wv(ve); heatv()
        VE(ve.tensor_scalar(tb, z, 128.0, 128.0, ALU.is_ge, ALU.mult))
        wv(ve)
        VE(ve.tensor_sub(r, z, tb))
        heatv()
        for b in (64.0, 32.0, 16.0, 8.0, 4.0, 2.0, 1.0):
            wv(ve)
            VE(ve.tensor_scalar(tb, r, b, b, ALU.is_ge, ALU.mult))
            wv(ve); heatv()
            VE(ve.tensor_sub(r, r, tb))
        wv(ve); heatv()
        VE(ve.tensor_sub(tb, z, r))
        wv(ve)
        VE(ve.tensor_scalar(dst, tb, -128.0, None, ALU.add))
        wv(ve); heatv()

    def _ckpt(k):
        if k >= PHASES:
            raise _StopBuild()

    try:
        _ckpt(0)
        with ExitStack() as ph:
            xt = ph.enter_context(nc.sbuf_tensor("s0x", [128, D], F32))
            xn = ph.enter_context(nc.sbuf_tensor("s0n", [128, D], F32))
            sq = ph.enter_context(nc.sbuf_tensor("s0q", [128, D], F32))
            scr = ph.enter_context(nc.sbuf_tensor("s0s", [128, 8], F32))
            for t in range(NT):
                wv(sy); wsc(sy)
                n = dmas(xt[:], x0_d[t * 128:(t + 1) * 128, :])
                ve.wait_ge(ds, n); sc.wait_ge(ds, n)
                ln_stats(xt[:], mu0[:, t:t + 1], rs0[:, t:t + 1], scr, sq[:])
                wdg(ve)
                apply_ln(xn[:], xt[:], mu0[:, t:t + 1], rs0[:, t:t + 1], sq[:])
                VE(ve.tensor_add(xn[:], xn[:], xt[:]))
                wv(gp)
                dmag(s_x1[t * 128:(t + 1) * 128, :], xn[:])
            fence()

        _ckpt(1)
        def rope_kappa(csb, dst, q, t, t1, t2):
            nh = 8
            cos = csb[:, t * 128: t * 128 + 64].unsqueeze(1).broadcast_to((128, nh, 64))
            sin1 = csb[:, t * 128 + 64: t * 128 + 96].unsqueeze(1).broadcast_to((128, nh, 32))
            sin2 = csb[:, t * 128 + 96: t * 128 + 128].unsqueeze(1).broadcast_to((128, nh, 32))
            q3 = q.rearrange("p (h d) -> p h d", h=nh)
            wv(ve)
            VE(ve.tensor_tensor(t1.rearrange("p (h d) -> p h d", h=nh), q3, cos, ALU.mult))
            VE(ve.scalar_tensor_tensor(
                t2.rearrange("p (h d) -> p h d", h=nh)[:, :, 0:32],
                q3[:, :, 32:64], -1.0, sin1, ALU.mult, ALU.mult))
            VE(ve.tensor_tensor(
                t2.rearrange("p (h d) -> p h d", h=nh)[:, :, 32:64],
                q3[:, :, 0:32], sin2, ALU.mult))
            wv(ve); heatv()
            VE(ve.tensor_add(t1, t1, t2))
            wv(ve)
            VE(ve.tensor_scalar(t2, t1, 0.0, None, ALU.min))
            wv(sc); wsc(sc)
            SC(sc.activation(dst, t2, AF.Exp))
            heats()
            wsc(ve)
            VE(ve.tensor_scalar(t2, t1, 0.0, None, ALU.max))
            wv(ve)
            VE(ve.tensor_add(dst, dst, t2))
            wv(ve); heatv()

        with ExitStack() as ph:
            fm = ph.enter_context(nc.sbuf_tensor("pa_fm", [128, NT * 512], F32))
            wsl = ph.enter_context(nc.sbuf_tensor("pa_w", [128, NT * 1024], F32))
            xtile = ph.enter_context(nc.sbuf_tensor("pa_x", [128, D], F32))
            ntile = ph.enter_context(nc.sbuf_tensor("pa_n", [128, D], F32))
            scr = ph.enter_context(nc.sbuf_tensor("pa_s", [128, D], F32))
            qt = ph.enter_context(nc.sbuf_tensor("pa_q", [128, 512], F32))
            kt = ph.enter_context(nc.sbuf_tensor("pa_k", [128, 512], F32))
            t1 = ph.enter_context(nc.sbuf_tensor("pa_1", [128, 512], F32))
            t2 = ph.enter_context(nc.sbuf_tensor("pa_2", [128, 512], F32))
            csb = ph.enter_context(nc.sbuf_tensor("pa_cs", [128, NT * 128], F32))
            dmas(csb[:].rearrange("p (n c) -> p n c", n=NT),
                 ccs_d.rearrange("(n p) c -> p n c", p=128))

            for pas in ("q01", "q23", "kv"):
                wt(sy)
                if pas == "q01":
                    nws = dmas_split(wsl, Wd["l0_wq"][:, 0:1024], 2048, 1024, parts=8)
                elif pas == "q23":
                    nws = dmas_split(wsl, Wd["l0_wq"][:, 1024:2048], 2048, 1024, parts=8)
                else:
                    nws = []
                    for q_ in range(4):
                        dmas(wsl[:].rearrange("p (n x c) -> p n x c", n=16, x=2)
                             [:, q_ * 4:(q_ + 1) * 4, 0, :],
                             Wd["l0_wk"][q_ * 512:(q_ + 1) * 512, :]
                             .rearrange("(n p) c -> p n c", p=128))
                        nws.append(dmas(
                            wsl[:].rearrange("p (n x c) -> p n x c", n=16, x=2)
                            [:, q_ * 4:(q_ + 1) * 4, 1, :],
                            Wd["l0_wv"][q_ * 512:(q_ + 1) * 512, :]
                            .rearrange("(n p) c -> p n c", p=128)))
                tn_wait_split(nws)
                for slab in range(4):
                    wt(ve)
                    build_fm(slab, fm, xtile, ntile, scr, mu0, rs0, x0_d)
                    evac_at = [0, 0, 0, 0]
                    for tl in range(4):
                        t = slab * 4 + tl
                        for nch in range(2):
                            bank = (tl * 2 + nch) % 4
                            tn.wait_ge(cv, evac_at[bank])
                            for k in range(NT):
                                (TN if k == NT - 1 else TNq)(
                                    tn.matmul(PS[bank][:, 0:512],
                                              fm[:, k * 512 + tl * 128: k * 512 + tl * 128 + 128],
                                              wsl[:, k * 1024 + nch * 512: k * 1024 + nch * 512 + 512],
                                              start=(k == 0), stop=(k == NT - 1)))
                            ve.wait_ge(ct, cnt["ct"])
                            ve.wait_ge(dg, cnt["dg"])
                            VE(ve.tensor_copy(qt[:], PS[bank][:, 0:512]))
                            evac_at[bank] = cnt["cv"]
                            if pas == "kv" and nch == 1:
                                wv(gp)
                                dmag(s_v[t * 128:(t + 1) * 128, :], qt[:])
                                continue
                            sc.wait_ge(dg, cnt["dg"])
                            ve.wait_ge(dg, cnt["dg"])
                            rope_kappa(csb, kt[:], qt[:], t, t1[:], t2[:])
                            if pas == "kv":
                                wv(gp)
                                dmag(s_kk[t * 128:(t + 1) * 128, :], kt[:])
                            else:
                                off = (0 if pas == "q01" else 1024) + nch * 512
                                wv(tn)
                                TN(tn.matmul(PS[6][0:1, 0:512], onest[:, 0:1], kt[:],
                                             start=True, stop=True))
                                wt(ve)
                                VE(ve.tensor_add(qgsum[0:1, off:off + 512],
                                                 qgsum[0:1, off:off + 512],
                                                 PS[6][0:1, 0:512]))
                                wv(gp)
                                dmag(s_qk[t * 128:(t + 1) * 128, off:off + 512], kt[:])
            fence()

        _ckpt(2)
        with ExitStack() as ph:
            kka = ph.enter_context(nc.sbuf_tensor("pb_kka", [128, NT * 512], F32))
            vta = ph.enter_context(nc.sbuf_tensor("pb_vta", [128, NT * 512], F32))
            prod = ph.enter_context(nc.sbuf_tensor("pb_p", [128, 2048], F32))
            lg = ph.enter_context(nc.sbuf_tensor("pb_l", [128, NT * 32], F32))
            lgT = ph.enter_context(nc.sbuf_tensor("pb_lt", [32, 2048], F32))
            sm = ph.enter_context(nc.sbuf_tensor("pb_sm", [32, 8], F32))
            ex = ph.enter_context(nc.sbuf_tensor("pb_ex", [32, 2048], F32))
            ak2 = ph.enter_context(nc.sbuf_tensor("pb_ak", [128, 512], F32))
            qgB = ph.enter_context(nc.sbuf_tensor("pb_qg", [128, 2048], F32))
            alsb = ph.enter_context(nc.sbuf_tensor("pb_al", [128, NT * 32], F32))

            wv(sy); wt(sy)
            dmas(kka[:].rearrange("p (n c) -> p n c", n=NT),
                 s_kk.rearrange("(n p) c -> p n c", p=128))
            nkv = dmas(vta[:].rearrange("p (n c) -> p n c", n=NT),
                       s_v.rearrange("(n p) c -> p n c", p=128))
            ve.wait_ge(ds, nkv)
            tn.wait_ge(ds, nkv)

            VE(ve.tensor_scalar(qgsum[:], qgsum[:], 1.0 / T, None, ALU.mult))
            wv(tn)
            for j in range(4):
                TN(tn.matmul(PS[j][0:128, 0:512], onest[0:1, 0:128],
                             qgsum[0:1, j * 512:(j + 1) * 512], start=True, stop=True))
            wt(ve)
            for j in range(4):
                VE(ve.tensor_copy(qgB[:, j * 512:(j + 1) * 512], PS[j][0:128, 0:512]))
            wv(ve)
            qg3 = qgB[:].rearrange("p (a c d) -> p a c d", a=8, c=4)
            for t in range(NT):
                kk4 = kka[:, t * 512:(t + 1) * 512].rearrange("p (h d) -> p h d", h=8) \
                    .unsqueeze(2).broadcast_to((128, 8, 4, 64))
                wv(ve)
                VE(ve.tensor_tensor(prod[:].rearrange("p (a c d) -> p a c d", a=8, c=4),
                                    kk4, qg3, ALU.mult))
                wv(ve); heatv()
                VE(ve.reduce_sum(lg[:, t * 32:(t + 1) * 32].rearrange("p (a c) -> p a c", a=8),
                                 prod[:].rearrange("p (a c d) -> p a c d", a=8, c=4),
                                 axis=AX))
                wv(tn)
                TN(tn.transpose(PS[5][0:32, 0:128], lg[:, t * 32:(t + 1) * 32], idt[:]))
                wt(ve)
                VE(ve.tensor_copy(lgT[:, t * 128:(t + 1) * 128], PS[5][0:32, 0:128]))
            wv(ve)
            VE(ve.reduce_max(sm[:, 0:1], lgT[:], axis=AX))
            wv(ve); heatv()
            VE(ve.tensor_scalar(sm[:, 1:2], sm[:, 0:1], -1.0, None, ALU.mult))
            wv(sc); wsc(sc)
            SC(sc.activation(ex[:], lgT[:], AF.Exp, bias=sm[:, 1:2], accum_out=sm[:, 2:3]))
            heats()
            wsc(ve)
            VE(ve.reciprocal(sm[:, 3:4], sm[:, 2:3]))
            wv(ve)
            VE(ve.tensor_scalar(sm[:, 4:5], sm[:, 3:4], float(T), None, ALU.mult))
            wv(ve)
            VE(ve.tensor_scalar(ex[:], ex[:], sm[:, 4:5], None, ALU.mult))
            wv(ve); heatv()
            for t in range(NT):
                wv(tn)
                TN(tn.transpose(PS[5][0:128, 0:32], ex[:, t * 128:(t + 1) * 128],
                                idt[0:32, 0:32]))
                wt(ve)
                VE(ve.tensor_copy(alsb[:, t * 32:(t + 1) * 32], PS[5][0:128, 0:32]))
            ev_bank = [cnt["cv"]] * 4
            last_half = [0, 0]
            for s in range(8):
                b0 = (s % 2) * 2
                for t in range(NT):
                    half = t % 2
                    ve.wait_ge(ct, last_half[half])
                    kkb = kka[:, t * 512 + s * 64: t * 512 + s * 64 + 64] \
                        .unsqueeze(1).broadcast_to((128, 4, 64))
                    alb = alsb[:, t * 32 + s * 4: t * 32 + s * 4 + 4] \
                        .unsqueeze(2).broadcast_to((128, 4, 64))
                    wv(ve)
                    VE(ve.tensor_tensor(ak2[:, half * 256: half * 256 + 256]
                                        .rearrange("p (c d) -> p c d", c=4),
                                        kkb, alb, ALU.mult))
                    wv(tn)
                    if t == 0:
                        tn.wait_ge(cv, ev_bank[b0])
                        tn.wait_ge(cv, ev_bank[b0 + 1])
                    for c in range(2):
                        (TN if c == 1 else TNq)(
                            tn.matmul(PS[b0 + c][0:128, 0:64],
                                      ak2[:, half * 256 + c * 128:
                                          half * 256 + c * 128 + 128],
                                      vta[:, t * 512 + s * 64: t * 512 + s * 64 + 64],
                                      start=(t == 0), stop=(t == NT - 1)))
                    last_half[half] = cnt["ct"]
                wt(ve)
                for c in range(2):
                    r = 2 * s + c
                    VE(ve.tensor_copy(osbd[0:64, r * 128: r * 128 + 64],
                                      PS[b0 + c][0:64, 0:64]))
                    VE(ve.tensor_copy(osbd[64:128, r * 128 + 64: r * 128 + 128],
                                      PS[b0 + c][64:128, 0:64]))
                ev_bank[b0] = cnt["cv"]
                ev_bank[b0 + 1] = cnt["cv"]
            fence()

        _ckpt(3)
        with ExitStack() as ph:
            qkT = ph.enter_context(nc.sbuf_tensor("pc_qt", [128, NT * 512], F32))
            res = ph.enter_context(nc.sbuf_tensor("pc_rs", [128, NT * 512], F32))
            xtile = ph.enter_context(nc.sbuf_tensor("pc_x", [128, D], F32))
            ntile = ph.enter_context(nc.sbuf_tensor("pc_n", [128, D], F32))
            scr = ph.enter_context(nc.sbuf_tensor("pc_s", [128, D], F32))
            wslab = ph.enter_context(nc.sbuf_tensor("pc_w", [128, NT * 512], F32))
            x1rw = ph.enter_context(nc.sbuf_tensor("pc_r", [128, 4 * 2048], F32))
            for slab in range(4):
                for tl in range(4):
                    t = slab * 4 + tl
                    wt(sy)
                    n = dmas(xtile[:], s_qk[t * 128:(t + 1) * 128, :])
                    tn.wait_ge(ds, n)
                    for k in range(NT):
                        wv(tn)
                        TN(tn.transpose(PS[5][:, 0:128], xtile[:, k * 128:(k + 1) * 128], idt[:]))
                        wt(ve)
                        VE(ve.tensor_copy(qkT[:, k * 512 + tl * 128: k * 512 + tl * 128 + 128],
                                          PS[5][:, 0:128]))
                ev = [0, 0, 0, 0]
                for r in range(NT):
                    bank = r % 4
                    tn.wait_ge(cv, ev[bank])
                    wv(tn)
                    TN(tn.matmul(PS[bank][:, 0:512], osbd[:, r * 128:(r + 1) * 128],
                                 qkT[:, r * 512:(r + 1) * 512], start=True, stop=True))
                    wt(ve)
                    VE(ve.tensor_copy(res[:, r * 512:(r + 1) * 512], PS[bank][:, 0:512]))
                    ev[bank] = cnt["cv"]
                wt(ve); wv(tn)
                build_fm(slab, qkT, xtile, ntile, scr, mu0, rs0, x0_d)
                for fs in range(4):
                    wt(sy)
                    nws = dmas_split(wslab, Wd["l0_wphi"][:, fs * 512:(fs + 1) * 512],
                                     2048, 512)
                    tn_wait_split(nws)
                    for fl in range(4):
                        f = fs * 4 + fl
                        bank = fl % 4
                        tn.wait_ge(cv, cnt["cv"])
                        for k in range(NT):
                            (TN if k == NT - 1 else TNq)(
                                tn.matmul(PS[bank][:, 0:512],
                                          wslab[:, k * 512 + fl * 128: k * 512 + fl * 128 + 128],
                                          qkT[:, k * 512:(k + 1) * 512],
                                          start=(k == 0), stop=(k == NT - 1)))
                        wt(ve)
                        VE(ve.tensor_tensor(res[:, f * 512:(f + 1) * 512],
                                            PS[bank][:, 0:512],
                                            res[:, f * 512:(f + 1) * 512], ALU.mult))
                wv(sy); wdg(sy)
                n = dmas(x1rw[:].rearrange("p (a c) -> p a c", a=4),
                         s_x1[slab * 512:(slab + 1) * 512, :]
                         .rearrange("(a p) c -> p a c", p=128))
                ve.wait_ge(ds, n)
                for nn in range(4):
                    wt(sy)
                    nws = dmas_split(wslab, Wd["l0_wo"][:, nn * 512:(nn + 1) * 512],
                                     2048, 512)
                    tn_wait_split(nws)
                    for m in range(4):
                        bank = m % 4
                        tn.wait_ge(cv, cnt["cv"])
                        wv(tn)
                        for k in range(NT):
                            (TN if k == NT - 1 else TNq)(
                                tn.matmul(PS[bank][:, 0:512],
                                          res[:, k * 512 + m * 128: k * 512 + m * 128 + 128],
                                          wslab[:, k * 512:(k + 1) * 512],
                                          start=(k == 0), stop=(k == NT - 1)))
                        wt(ve)
                        VE(ve.tensor_add(x1rw[:, m * 2048 + nn * 512: m * 2048 + nn * 512 + 512],
                                         x1rw[:, m * 2048 + nn * 512: m * 2048 + nn * 512 + 512],
                                         PS[bank][:, 0:512]))
                wv(gp); wdg(gp)
                dmag(s_x1[slab * 512:(slab + 1) * 512, :]
                     .rearrange("(a p) c -> p a c", p=128),
                     x1rw[:].rearrange("p (a c) -> p a c", a=4))
                gp.wait_ge(dg, cnt["dg"])
            fence()

        _ckpt(4)
        with ExitStack() as ph:
            wch = ph.enter_context(nc.sbuf_tensor("fw_w", [128, 2 * 8192], F32))
            ac = ph.enter_context(nc.sbuf_tensor("fw_a", [128, 4], F32))
            tot = ph.enter_context(nc.sbuf_tensor("fw_t", [1, 2], F32))
            VE(ve.memset(ac[:], 0.0))
            chunks = [("gv", c) for c in range(0, 128, 4)] + \
                     [("wo", c) for c in range(NT)]
            ld_at = [0, 0]

            def _fw_load(i):
                kind, c = chunks[i]
                half = i % 2
                dst = wch[:, half * 8192:(half + 1) * 8192]
                (sy if half == 0 else gp).wait_ge(cv, ld_at[half])
                do = dmas if half == 0 else dmag
                if kind == "gv":
                    return do(dst.rearrange("p (a c) -> p a c", a=4),
                              Wd["wgv"][c * 128:(c + 4) * 128, :]
                              .rearrange("(a p) c -> p a c", p=128))
                return do(dst, Wd["wout"][c * 128:(c + 1) * 128, :])
            pend = [_fw_load(0), _fw_load(1)]
            for i in range(len(chunks)):
                half = i % 2
                kind, c = chunks[i]
                ve.wait_ge(ds if half == 0 else dg, pend[half])
                col, accc = (1, 0) if kind == "gv" else (3, 2)
                for q in range(4):
                    wv(ve)
                    VE(ve.tensor_reduce(ac[:, col:col + 1],
                                        wch[:, half * 8192 + q * 2048:
                                            half * 8192 + (q + 1) * 2048],
                                        AX, ALU.add, apply_absolute_value=True))
                    wv(ve)
                    VE(ve.tensor_add(ac[:, accc:accc + 1], ac[:, accc:accc + 1],
                                     ac[:, col:col + 1]))
                    heatv()
                ld_at[half] = cnt["cv"]
                if i + 2 < len(chunks):
                    pend[half] = _fw_load(i + 2)
            wv(tn)
            TN(tn.matmul(PS[6][0:1, 0:2], onest[:, 0:1],
                         ac[:].rearrange("p (a c) -> p a c", a=2)[:, :, 0:1]
                         .squeeze(2), start=True, stop=True))
            wt(ve)
            VE(ve.tensor_copy(tot[0:1, 0:2], PS[6][0:1, 0:2]))
            wv(tn)
            TN(tn.matmul(PS[5][0:128, 0:2], onest[0:1, 0:128], tot[0:1, 0:2],
                         start=True, stop=True))
            wt(ve)
            VE(ve.tensor_scalar(mgv[:], PS[5][0:128, 0:1], 1.0 / (16384.0 * 2048.0),
                                None, ALU.mult))
            VE(ve.tensor_scalar(mwo[:], PS[5][0:128, 1:2], 1.0 / (2048.0 * 8192.0),
                                None, ALU.mult))
            wv(ve)
            VE(ve.tensor_scalar(thrgv[:], mgv[:], 0.5, None, ALU.mult))
            VE(ve.tensor_scalar(thrwo[:], mwo[:], 0.5, None, ALU.mult))
            VE(ve.tensor_scalar(mgv[:], mgv[:], EPS, None, ALU.max))
            wv(ve)
            VE(ve.tensor_scalar(mwo[:], mwo[:], EPS, None, ALU.max))
            wv(ve)
            fence()

        _ckpt(5)
        with ExitStack() as ph:
            hqT = ph.enter_context(nc.sbuf_tensor("fh_t", [128, NT * 2048], BF16))
            with ExitStack() as ph1:
                xt = ph1.enter_context(nc.sbuf_tensor("fh_x", [128, D], F32))
                ht = ph1.enter_context(nc.sbuf_tensor("fh_h", [128, D], F32))
                hq = ph1.enter_context(nc.sbuf_tensor("fh_q", [128, D], BF16))
                z = ph1.enter_context(nc.sbuf_tensor("fh_z", [128, D], F32))
                r = ph1.enter_context(nc.sbuf_tensor("fh_r", [128, D], F32))
                tb = ph1.enter_context(nc.sbuf_tensor("fh_b", [128, D], F32))
                sq = ph1.enter_context(nc.sbuf_tensor("fh_sq", [128, D], F32))
                s8 = ph1.enter_context(nc.sbuf_tensor("fh_s", [128, 8], F32))
                wnormB = ph1.enter_context(nc.sbuf_tensor("fh_wn", [128, D], F32))
                dmag(wnormB[:], Wd["wnorm"][0:1, :].partition_broadcast(128))
                wdg(ve)
                for t in range(NT):
                    wv(sy); wsc(sy)
                    n = dmas(xt[:], x0_d[t * 128:(t + 1) * 128, :])
                    ve.wait_ge(ds, n); sc.wait_ge(ds, n)
                    apply_ln(ht[:], xt[:], mu0[:, t:t + 1], rs0[:, t:t + 1], sq[:])
                    wv(sc); wsc(sc)
                    SC(sc.activation(sq[:], ht[:], AF.Square, accum_out=s8[:, 0:1]))
                    heats()
                    wsc(ve)
                    VE(ve.tensor_scalar(s8[:, 1:2], s8[:, 0:1], 1.0 / D, None, ALU.mult))
                    wv(sc)
                    SC(sc.activation(s8[:, 2:3], s8[:, 1:2], AF.Sqrt, bias=epst[:, 0:1]))
                    wsc(ve)
                    VE(ve.reciprocal(s8[:, 3:4], s8[:, 2:3]))
                    wv(ve)
                    VE(ve.tensor_scalar(ht[:], ht[:], s8[:, 3:4], None, ALU.mult))
                    wv(ve); heatv()
                    VE(ve.tensor_tensor(ht[:], ht[:], wnormB[:], ALU.mult))
                    wv(ve)
                    VE(ve.reduce_max(s8[:, 4:5], ht[:], axis=AX, apply_absolute_value=True))
                    wv(ve); heatv()
                    VE(ve.tensor_scalar(s8[:, 4:5], s8[:, 4:5], EPS, None, ALU.max))
                    wv(ve)
                    VE(ve.reciprocal(s8[:, 5:6], s8[:, 4:5]))
                    wv(ve)
                    VE(ve.tensor_scalar(s8[:, 6:7], s8[:, 5:6], 127.0, None, ALU.mult))
                    wv(ve)
                    VE(ve.reciprocal(s8[:, 7:8], s8[:, 6:7]))
                    wv(ve)
                    VE(ve.tensor_scalar(fgv[:, t:t + 1], s8[:, 7:8], mgv[:, 0:1], None, ALU.mult))
                    VE(ve.tensor_scalar(z[:], ht[:], s8[:, 6:7], None, ALU.mult))
                    wv(ve)
                    binary_round(hq[:], z[:], sq[:], r[:], tb[:])
                    for k in range(NT):
                        wv(tn)
                        TN(tn.transpose(PSB[:, 0:128], hq[:, k * 128:(k + 1) * 128], idb[:]))
                        wt(ve)
                        VE(ve.tensor_copy(hqT[:, k * 2048 + t * 128: k * 2048 + t * 128 + 128],
                                          PSB[:, 0:128]))
                fence()
            with ExitStack() as ph2:
                wraw = ph2.enter_context(nc.sbuf_tensor("fg_w", [128, 4 * 2048], F32))
                wq = ph2.enter_context(nc.sbuf_tensor("fg_q", [128, 4 * 2048], BF16))
                stp = ph2.enter_context(nc.sbuf_tensor("fg_st", [128, 4 * 2048], F32))
                wTv = ph2.enter_context(nc.sbuf_tensor("fg_tv", [128, NT * 512], BF16))
                wTg = ph2.enter_context(nc.sbuf_tensor("fg_tg", [128, NT * 512], BF16))
                sg = ph2.enter_context(nc.sbuf_tensor("fg_s", [128, 512], F32))
                ut = ph2.enter_context(nc.sbuf_tensor("fg_u", [128, 512], F32))
                rm = ph2.enter_context(nc.sbuf_tensor("fg_rm", [128, 1], F32))

                def quant_w_to(dstT, og, thr):
                    wv(sy); wsc(sy); wt(sy)
                    nlo = dmas(wraw[:, 0:4096].rearrange("p (a c) -> p a c", a=2),
                               Wd["wgv"][og * 512: og * 512 + 256, :]
                               .rearrange("(a p) c -> p a c", p=128))
                    nhi = dmas(wraw[:, 4096:8192].rearrange("p (a c) -> p a c", a=2),
                               Wd["wgv"][og * 512 + 256:(og + 1) * 512, :]
                               .rearrange("(a p) c -> p a c", p=128))
                    for hf_ in range(2):
                        n_ = nlo if hf_ == 0 else nhi
                        sl = slice(hf_ * 4096, (hf_ + 1) * 4096)
                        ve.wait_ge(ds, n_); sc.wait_ge(ds, n_)
                        tn.wait_ge(ds, n_); heatmm()
                        wv(sc); wsc(sc)
                        SC(sc.activation(stp[:, sl], wraw[:, sl], AF.Abs))
                        heats()
                        wsc(ve); wv(ve)
                        VE(ve.tensor_scalar(stp[:, sl], stp[:, sl], thr[:, 0:1],
                                            None, ALU.is_ge))
                        wv(ve); heatv()
                        VE(ve.tensor_tensor(wraw[:, sl], wraw[:, sl], stp[:, sl],
                                            ALU.mult))
                        wv(sc); wsc(sc); heatv()
                        SC(sc.activation(wq[:, sl], wraw[:, sl], AF.Sign))
                        heats()
                    wsc(tn); wsc(ve)
                    for a in range(4):
                        for k in range(NT):
                            wv(tn)
                            TN(tn.transpose(PSB[:, 0:128],
                                            wq[:, a * 2048 + k * 128: a * 2048 + k * 128 + 128],
                                            idb[:]))
                            wt(ve)
                            VE(ve.tensor_copy(
                                dstT[:, k * 512 + a * 128: k * 512 + a * 128 + 128],
                                PSB[:, 0:128]))

                for og in range(16):
                    quant_w_to(wTv, og, thrgv)
                    quant_w_to(wTg, og + 16, thrgv)
                    for t in range(NT):
                        pv = PS[t % 2]
                        pg = PS[2 + t % 2]
                        tn.wait_ge(cv, cnt["cv"])
                        for k in range(NT):
                            (TN if k == NT - 1 else TNq)(
                                tn.matmul(pv[:, 0:512],
                                          hqT[:, k * 2048 + t * 128: k * 2048 + t * 128 + 128],
                                          wTv[:, k * 512:(k + 1) * 512],
                                          start=(k == 0), stop=(k == NT - 1)))
                        for k in range(NT):
                            (TN if k == NT - 1 else TNq)(
                                tn.matmul(pg[:, 0:512],
                                          hqT[:, k * 2048 + t * 128: k * 2048 + t * 128 + 128],
                                          wTg[:, k * 512:(k + 1) * 512],
                                          start=(k == 0), stop=(k == NT - 1)))
                        ve.wait_ge(ct, cnt["ct"])
                        sc.wait_ge(ct, cnt["ct"])
                        ve.wait_ge(dg, cnt["dg"])
                        sc.wait_ge(dg, cnt["dg"])
                        VE(ve.tensor_scalar(sg[:], pg[:, 0:512], fgv[:, t:t + 1], None, ALU.mult))
                        wv(sc); wsc(sc)
                        SC(sc.activation(sg[:], sg[:], AF.Silu))
                        VE(ve.tensor_scalar(ut[:], pv[:, 0:512], fgv[:, t:t + 1], None, ALU.mult))
                        wsc(ve); wv(ve)
                        VE(ve.tensor_tensor(ut[:], ut[:], sg[:], ALU.mult))
                        wv(ve); heatv()
                        VE(ve.reduce_max(rm[:], ut[:], axis=AX, apply_absolute_value=True))
                        wv(ve)
                        VE(ve.tensor_tensor(umax[:, t:t + 1], umax[:, t:t + 1], rm[:], ALU.max))
                        wv(gp)
                        dmag(s_u[t * 128:(t + 1) * 128, og * 512:(og + 1) * 512], ut[:])
                fence()

        _ckpt(6)
        with ExitStack() as ph:
            wraw = ph.enter_context(nc.sbuf_tensor("fp_w", [128, 8192], F32))
            stp = ph.enter_context(nc.sbuf_tensor("fp_st", [128, 8192], F32))
            wq = ph.enter_context(nc.sbuf_tensor("fp_q", [128, 8192], BF16))
            col = ph.enter_context(nc.sbuf_tensor("fp_c", [128, 64 * 128], BF16))
            for g in range(16):
                wv(sy); wsc(sy); wdg(sy)
                nlo = dmas(wraw[:, 0:4096], Wd["wout"][g * 128:(g + 1) * 128, 0:4096])
                nhi = dmas(wraw[:, 4096:8192],
                           Wd["wout"][g * 128:(g + 1) * 128, 4096:8192])
                for hf_ in range(2):
                    n_ = nlo if hf_ == 0 else nhi
                    sl = slice(hf_ * 4096, (hf_ + 1) * 4096)
                    ve.wait_ge(ds, n_); sc.wait_ge(ds, n_)
                    tn.wait_ge(ds, n_); heatmm()
                    wv(sc); wsc(sc)
                    SC(sc.activation(stp[:, sl], wraw[:, sl], AF.Abs))
                    heats()
                    wsc(ve); wv(ve)
                    VE(ve.tensor_scalar(stp[:, sl], stp[:, sl], thrwo[:, 0:1],
                                        None, ALU.is_ge))
                    wv(ve); heatv()
                    VE(ve.tensor_tensor(wraw[:, sl], wraw[:, sl], stp[:, sl],
                                        ALU.mult))
                    wv(sc); wsc(sc); heatv()
                    SC(sc.activation(wq[:, sl], wraw[:, sl], AF.Sign))
                    heats()
                wsc(tn); wsc(ve)
                wdg(ve)
                for k in range(64):
                    wv(tn)
                    TN(tn.transpose(PSB[:, 0:128], wq[:, k * 128:(k + 1) * 128], idb[:]))
                    wt(ve)
                    VE(ve.tensor_copy(col[:, k * 128:(k + 1) * 128], PSB[:, 0:128]))
                wv(gp)
                dmag(s_wo[g // 4, :, :, (g % 4) * 128:(g % 4) * 128 + 128]
                     .rearrange("k p c -> p k c"),
                     col[:].rearrange("p (k c) -> p k c", k=64))
            fence()

        _ckpt(7)
        with ExitStack() as ph:
            ufm = ph.enter_context(nc.sbuf_tensor("fu_t", [128, 64 * 512], BF16))
            utile = ph.enter_context(nc.sbuf_tensor("fu_u", [128, 8192], F32))
            r = ph.enter_context(nc.sbuf_tensor("fu_r", [128, 4096], F32))
            tb = ph.enter_context(nc.sbuf_tensor("fu_b", [128, 4096], F32))
            uq = ph.enter_context(nc.sbuf_tensor("fu_q", [128, 8192], BF16))
            s8 = ph.enter_context(nc.sbuf_tensor("fu_s", [128, 8], F32))
            x1rw = ph.enter_context(nc.sbuf_tensor("fu_x", [128, 4 * 2048], F32))
            rhs = [ph.enter_context(nc.sbuf_tensor(f"fu_w{i}", [128, 4 * 512], BF16))
                   for i in range(2)]
            for slab in range(4):
                for tl in range(4):
                    t = slab * 4 + tl
                    wv(sy)
                    n = dmas(utile[:], s_u[t * 128:(t + 1) * 128, :])
                    ve.wait_ge(ds, n)
                    wv(ve)
                    VE(ve.tensor_scalar(s8[:, 0:1], umax[:, t:t + 1], EPS, None, ALU.max))
                    wv(ve)
                    VE(ve.reciprocal(s8[:, 1:2], s8[:, 0:1]))
                    wv(ve)
                    VE(ve.tensor_scalar(s8[:, 2:3], s8[:, 1:2], 127.0, None, ALU.mult))
                    wv(ve)
                    VE(ve.reciprocal(s8[:, 3:4], s8[:, 2:3]))
                    wv(ve)
                    VE(ve.tensor_scalar(fwo[:, t:t + 1], s8[:, 3:4], mwo[:, 0:1], None, ALU.mult))
                    VE(ve.tensor_scalar(utile[:], utile[:], s8[:, 2:3], None, ALU.mult))
                    wv(ve)
                    for hf in range(2):
                        hs = slice(hf * 4096, (hf + 1) * 4096)
                        binary_round(uq[:, hs], utile[:, hs], utile[:, hs], r[:], tb[:])
                    for k in range(64):
                        wv(tn)
                        TN(tn.transpose(PSB[:, 0:128], uq[:, k * 128:(k + 1) * 128], idb[:]))
                        wt(ve)
                        VE(ve.tensor_copy(ufm[:, k * 512 + tl * 128: k * 512 + tl * 128 + 128],
                                          PSB[:, 0:128]))
                wv(sy); wdg(sy)
                n = dmas(x1rw[:].rearrange("p (a c) -> p a c", a=4),
                         s_x1[slab * 512:(slab + 1) * 512, :]
                         .rearrange("(a p) c -> p a c", p=128))
                ve.wait_ge(ds, n)
                for nn in range(4):
                    ev_at = cnt["cv"]
                    mm_after = {}
                    for hk8 in range(16):
                        buf = rhs[hk8 % 2]
                        if hk8 >= 2:
                            sy.wait_ge(ct, mm_after[hk8 - 2])
                        nr = dmas(buf[:].rearrange("p (k c) -> p k c", k=4),
                                  s_wo[nn, hk8 * 4:(hk8 + 1) * 4, :, :]
                                  .rearrange("k p c -> p k c"))
                        tn.wait_ge(ds, nr)
                        if hk8 == 0:
                            tn.wait_ge(cv, ev_at)
                        for k8 in range(4):
                            hk = hk8 * 4 + k8
                            for m in range(4):
                                (TN if k8 == 3 and m == 3 else TNq)(
                                    tn.matmul(PS[m][:, 0:512],
                                              ufm[:, hk * 512 + m * 128: hk * 512 + m * 128 + 128],
                                              buf[:, k8 * 512:(k8 + 1) * 512],
                                              start=(hk == 0), stop=(hk == 63)))
                        mm_after[hk8] = cnt["ct"]
                    wt(ve)
                    for m in range(4):
                        t = slab * 4 + m
                        wv(ve)
                        VE(ve.tensor_scalar(tb[:, 0:512], PS[m][:, 0:512],
                                            fwo[:, t:t + 1], None, ALU.mult))
                        wv(ve)
                        VE(ve.tensor_add(x1rw[:, m * 2048 + nn * 512: m * 2048 + nn * 512 + 512],
                                         x1rw[:, m * 2048 + nn * 512: m * 2048 + nn * 512 + 512],
                                         tb[:, 0:512]))
                wv(gp); wdg(gp)
                dmag(s_x1[slab * 512:(slab + 1) * 512, :]
                     .rearrange("(a p) c -> p a c", p=128),
                     x1rw[:].rearrange("p (a c) -> p a c", a=4))
                gp.wait_ge(dg, cnt["dg"])
            fence()

        _ckpt(8)
        with ExitStack() as ph:
            xt = ph.enter_context(nc.sbuf_tensor("s1x", [128, D], F32))
            sq = ph.enter_context(nc.sbuf_tensor("s1q", [128, D], F32))
            scr = ph.enter_context(nc.sbuf_tensor("s1s", [128, 8], F32))
            for t in range(NT):
                wv(sy); wsc(sy)
                n = dmas(xt[:], s_x1[t * 128:(t + 1) * 128, :])
                ve.wait_ge(ds, n); sc.wait_ge(ds, n)
                ln_stats(xt[:], mu1[:, t:t + 1], rs1[:, t:t + 1], scr, sq[:])
            fence()

        _ckpt(9)
        with ExitStack() as ph:
            kvtm = ph.enter_context(nc.sbuf_tensor("la_kv", [128, NT * 512], F32))
            kfm = ph.enter_context(nc.sbuf_tensor("la_kf", [128, 2 * 2048], F32))
            fm = ph.enter_context(nc.sbuf_tensor("la_fm", [128, NT * 512], F32))
            wsl = ph.enter_context(nc.sbuf_tensor("la_w", [128, NT * 512], F32))
            xtile = ph.enter_context(nc.sbuf_tensor("la_x", [128, D], F32))
            qfm = ph.enter_context(nc.sbuf_tensor("la_qf", [128, NT * 512], F32))
            sct = ph.enter_context(nc.sbuf_tensor("la_sc", [128, 2048], F32))
            ptT = ph.enter_context(nc.sbuf_tensor("la_pt", [128, NT * 128], F32))
            otile = ph.enter_context(nc.sbuf_tensor("la_o", [128, 2048], F32))
            ntile = otile
            scr = sct
            s8 = ph.enter_context(nc.sbuf_tensor("la_8", [128, 8], F32))
            bkvB = ph.enter_context(nc.sbuf_tensor("la_bk", [128, 512], F32))
            dmag(bkvB[:, 0:256], Wd["l1_bk"][0:1, :].partition_broadcast(128))
            dmag(bkvB[:, 256:512], Wd["l1_bv"][0:1, :].partition_broadcast(128))
            wdg(ve)

            nw = dmas(wsl[:].rearrange("p (n x c) -> p n x c", n=16, x=2)[:, :, 0, :],
                      Wd["l1_wk"].rearrange("(n p) c -> p n c", p=128))
            nw = dmas(wsl[:].rearrange("p (n x c) -> p n x c", n=16, x=2)[:, :, 1, :],
                      Wd["l1_wv"].rearrange("(n p) c -> p n c", p=128))
            tn.wait_ge(ds, nw)
            for slab in range(4):
                wt(ve)
                build_fm(slab, fm, xtile, ntile, scr, mu1, rs1, s_x1)
                for tl in range(4):
                    t = slab * 4 + tl
                    bank = tl % 2
                    tn.wait_ge(cv, cnt["cv"])
                    for k in range(NT):
                        (TN if k == NT - 1 else TNq)(
                            tn.matmul(PS[bank][:, 0:512],
                                      fm[:, k * 512 + tl * 128: k * 512 + tl * 128 + 128],
                                      wsl[:, k * 512:(k + 1) * 512],
                                      start=(k == 0), stop=(k == NT - 1)))
                    wt(ve)
                    VE(ve.tensor_add(kvtm[:, t * 512:(t + 1) * 512], PS[bank][:, 0:512],
                                     bkvB[:]))
            for t in range(NT):
                for kvh in range(2):
                    wv(tn)
                    TN(tn.transpose(PS[5][:, 0:128],
                                    kvtm[:, t * 512 + kvh * 128: t * 512 + kvh * 128 + 128],
                                    idt[:]))
                    wt(ve)
                    VE(ve.tensor_copy(kfm[:, kvh * 2048 + t * 128: kvh * 2048 + t * 128 + 128],
                                      PS[5][:, 0:128]))
            for slab in range(2):
                wt(ve)
                build_fm(slab, fm, xtile, ntile, scr, mu1, rs1, s_x1)
                for qg in range(4):
                    wt(sy)
                    nws = dmas_split(wsl, Wd["l1_wq"][:, qg * 512:(qg + 1) * 512],
                                     2048, 512)
                    tn_wait_split(nws)
                    tn.wait_ge(cv, cnt["cv"])
                    for m in range(4):
                        for k in range(NT):
                            (TN if k == NT - 1 else TNq)(
                                tn.matmul(PS[m][:, 0:512],
                                          wsl[:, k * 512 + m * 128: k * 512 + m * 128 + 128],
                                          fm[:, k * 512:(k + 1) * 512],
                                          start=(k == 0), stop=(k == NT - 1)))
                    wt(ve)
                    for m in range(4):
                        h = qg * 4 + m
                        VE(ve.tensor_scalar(qfm[:, h * 512:(h + 1) * 512], PS[m][:, 0:512],
                                            bqc[:, h:h + 1], None, ALU.add))
                for tl in range(4):
                    for h in range(16):
                        kv = h // 8
                        for sch in range(4):
                            bank = 4 + sch % 2
                            tn.wait_ge(cv, cnt["cv"])
                            wv(tn)
                            TN(tn.matmul(PS[bank][:, 0:512],
                                         qfm[:, h * 512 + tl * 128: h * 512 + tl * 128 + 128],
                                         kfm[:, kv * 2048 + sch * 512: kv * 2048 + sch * 512 + 512],
                                         start=True, stop=True))
                            wt(ve)
                            VE(ve.tensor_scalar(sct[:, sch * 512:(sch + 1) * 512],
                                                PS[bank][:, 0:512], SCALE1, -100.0,
                                                ALU.mult, ALU.max))
                        wv(ve)
                        VE(ve.tensor_scalar(sct[:], sct[:], 100.0, None, ALU.min))
                        wv(ve); heatv()
                        VE(ve.reduce_max(s8[:, 0:1], sct[:], axis=AX))
                        wv(ve)
                        VE(ve.tensor_scalar(s8[:, 1:2], s8[:, 0:1], -1.0, None, ALU.mult))
                        wv(sc); wsc(sc); heatv()
                        SC(sc.activation(sct[:], sct[:], AF.Exp, bias=s8[:, 1:2],
                                         accum_out=s8[:, 2:3]))
                        heats()
                        wsc(ve)
                        VE(ve.reciprocal(s8[:, 3:4], s8[:, 2:3]))
                        wv(ve)
                        VE(ve.tensor_scalar(sct[:], sct[:], s8[:, 3:4], None, ALU.mult))
                        wv(ve); heatv()
                        for k in range(NT):
                            wv(tn)
                            TN(tn.transpose(PS[5][:, 0:128], sct[:, k * 128:(k + 1) * 128],
                                            idt[:]))
                            wt(ve)
                            VE(ve.tensor_copy(ptT[:, k * 128:(k + 1) * 128], PS[5][:, 0:128]))
                        wv(tn); tn.wait_ge(cv, cnt["cv"])
                        for k in range(NT):
                            (TN if k == NT - 1 else TNq)(
                                tn.matmul(PS[6][:, 0:128], ptT[:, k * 128:(k + 1) * 128],
                                          kvtm[:, k * 512 + 256 + kv * 128:
                                               k * 512 + 256 + kv * 128 + 128],
                                          start=(k == 0), stop=(k == NT - 1)))
                        wt(ve)
                        ve.wait_ge(dg, cnt["dg"])
                        VE(ve.tensor_copy(otile[:, h * 128:(h + 1) * 128], PS[6][:, 0:128]))
                    wv(gp)
                    dmag(s_att[(slab * 4 + tl) * 128:(slab * 4 + tl + 1) * 128, :], otile[:])
            fence()

        _ckpt(10)
        with ExitStack() as ph:
            ofm = ph.enter_context(nc.sbuf_tensor("w1_f", [128, NT * 512], F32))
            atile = ph.enter_context(nc.sbuf_tensor("w1_a", [128, 2048], F32))
            wslab = ph.enter_context(nc.sbuf_tensor("w1_w", [128, NT * 512], F32))
            x1rw = ph.enter_context(nc.sbuf_tensor("w1_x", [128, 4 * 2048], F32))
            tmp = ph.enter_context(nc.sbuf_tensor("w1_t", [128, 512], F32))
            boB = ph.enter_context(nc.sbuf_tensor("w1_bo", [128, 2048], F32))
            dmag(boB[:], Wd["l1_bo"][0:1, :].partition_broadcast(128))
            wdg(ve)
            for slab in range(2):
                for tl in range(4):
                    t = slab * 4 + tl
                    wt(sy)
                    n = dmas(atile[:], s_att[t * 128:(t + 1) * 128, :])
                    tn.wait_ge(ds, n)
                    for k in range(NT):
                        wv(tn)
                        TN(tn.transpose(PS[5][:, 0:128], atile[:, k * 128:(k + 1) * 128], idt[:]))
                        wt(ve)
                        VE(ve.tensor_copy(ofm[:, k * 512 + tl * 128: k * 512 + tl * 128 + 128],
                                          PS[5][:, 0:128]))
                wv(sy); wdg(sy)
                n = dmas(x1rw[:].rearrange("p (a c) -> p a c", a=4),
                         s_x1[slab * 512:(slab + 1) * 512, :]
                         .rearrange("(a p) c -> p a c", p=128))
                ve.wait_ge(ds, n)
                for nn in range(4):
                    wt(sy)
                    nws = dmas_split(wslab, Wd["l1_wo"][:, nn * 512:(nn + 1) * 512],
                                     2048, 512)
                    tn_wait_split(nws)
                    for m in range(4):
                        bank = m % 4
                        tn.wait_ge(cv, cnt["cv"])
                        wv(tn)
                        for k in range(NT):
                            (TN if k == NT - 1 else TNq)(
                                tn.matmul(PS[bank][:, 0:512],
                                          ofm[:, k * 512 + m * 128: k * 512 + m * 128 + 128],
                                          wslab[:, k * 512:(k + 1) * 512],
                                          start=(k == 0), stop=(k == NT - 1)))
                        wt(ve); wv(ve)
                        VE(ve.tensor_add(tmp[:], PS[bank][:, 0:512],
                                         boB[:, nn * 512:(nn + 1) * 512]))
                        wv(ve)
                        VE(ve.tensor_add(x1rw[:, m * 2048 + nn * 512: m * 2048 + nn * 512 + 512],
                                         x1rw[:, m * 2048 + nn * 512: m * 2048 + nn * 512 + 512],
                                         tmp[:]))
                wv(gp); wdg(gp)
                dmag(s_y[slab * 512:(slab + 1) * 512, :]
                     .rearrange("(a p) c -> p a c", p=128),
                     x1rw[:].rearrange("p (a c) -> p a c", a=4))
                gp.wait_ge(dg, cnt["dg"])
            fence()

        _ckpt(11)
        with ExitStack() as ph:
            fm = ph.enter_context(nc.sbuf_tensor("mo_fm", [128, NT * 512], F32))
            acc = ph.enter_context(nc.sbuf_tensor("mo_ac", [128, 4 * 2048], F32))
            gt = ph.enter_context(nc.sbuf_tensor("mo_g", [128, 8], F32))
            g8 = ph.enter_context(nc.sbuf_tensor("mo_8", [128, 24], F32))
            wgsb = ph.enter_context(nc.sbuf_tensor("mo_wg", [128, NT * 8], F32))
            beB = ph.enter_context(nc.sbuf_tensor("mo_be", [128, 2048], F32))
            bdB = ph.enter_context(nc.sbuf_tensor("mo_bd", [128, 2048], F32))
            for g in range(2):
                with ExitStack() as phx:
                    xtile = phx.enter_context(nc.sbuf_tensor(f"mo_x{g}", [128, D], F32))
                    ntile = phx.enter_context(nc.sbuf_tensor(f"mo_n{g}", [128, D], F32))
                    scr = phx.enter_context(nc.sbuf_tensor(f"mo_s{g}", [128, D], F32))
                    wt(ve)
                    build_fm(g, fm, xtile, ntile, scr, mu1, rs1, s_x1)
                    fence()
                wt(sy)
                nw = dmas(wgsb[:].rearrange("p (n c) -> p n c", n=16), Wd["wg"].rearrange("(n p) c -> p n c", p=128))
                tn.wait_ge(ds, nw)
                for m in range(4):
                    tn.wait_ge(cv, cnt["cv"])
                    wv(tn)
                    for k in range(NT):
                        (TN if k == NT - 1 else TNq)(
                            tn.matmul(PS[6][:, 0:8],
                                      fm[:, k * 512 + m * 128: k * 512 + m * 128 + 128],
                                      wgsb[:, k * 8:(k + 1) * 8],
                                      start=(k == 0), stop=(k == NT - 1)))
                    wt(ve); wv(ve)
                    VE(ve.tensor_add(gt[:], PS[6][:, 0:8], bgB[:]))
                    wv(ve)
                    VE(ve.tensor_reduce(g8[:, 0:1], gt[:], AX, ALU.min))
                    wv(ve)
                    VE(ve.tensor_scalar(g8[:, 8:16], gt[:], g8[:, 0:1], None, ALU.is_equal))
                    wv(ve)
                    VE(ve.tensor_tensor_scan(g8[:, 16:24], onest[:, 0:8], g8[:, 8:16],
                                             0.0, ALU.mult, ALU.add))
                    wv(ve)
                    VE(ve.reduce_sum(g8[:, 1:2], g8[:, 8:16], axis=AX))
                    wv(ve)
                    VE(ve.tensor_scalar(g8[:, 16:24], g8[:, 16:24], g8[:, 1:2], None,
                                        ALU.is_equal))
                    wv(ve)
                    VE(ve.tensor_tensor(g8[:, 8:16], g8[:, 8:16], g8[:, 16:24], ALU.mult))
                    wv(ve)
                    VE(ve.scalar_tensor_tensor(g8[:, 16:24], g8[:, 8:16], 1e30, gt[:],
                                               ALU.mult, ALU.add))
                    wv(ve)
                    VE(ve.tensor_reduce(g8[:, 2:3], g8[:, 16:24], AX, ALU.min))
                    wv(ve)
                    VE(ve.tensor_scalar(g8[:, 16:24], g8[:, 16:24], g8[:, 2:3], None,
                                        ALU.is_equal))
                    wv(ve)
                    VE(ve.tensor_tensor_scan(g8[:, 0:8], onest[:, 0:8], g8[:, 16:24],
                                             0.0, ALU.mult, ALU.add))
                    wv(ve)
                    VE(ve.reduce_sum(g8[:, 3:4], g8[:, 16:24], axis=AX))
                    wv(ve)
                    VE(ve.tensor_scalar(g8[:, 0:8], g8[:, 0:8], g8[:, 3:4], None,
                                        ALU.is_equal))
                    wv(ve)
                    VE(ve.tensor_tensor(g8[:, 16:24], g8[:, 16:24], g8[:, 0:8], ALU.mult))
                    wv(ve)
                    VE(ve.tensor_add(g8[:, 8:16], g8[:, 8:16], g8[:, 16:24]))
                    wv(ve)
                    VE(ve.scalar_tensor_tensor(g8[:, 16:24], g8[:, 8:16], -1e30, gt[:],
                                               ALU.mult, ALU.add))
                    wv(ve)
                    VE(ve.reduce_max(g8[:, 4:5], g8[:, 16:24], axis=AX))
                    wv(ve)
                    VE(ve.tensor_scalar(g8[:, 5:6], g8[:, 4:5], -1.0, None, ALU.mult))
                    wv(sc); wsc(sc)
                    SC(sc.activation(g8[:, 16:24], g8[:, 16:24], AF.Exp, bias=g8[:, 5:6],
                                     accum_out=g8[:, 6:7]))
                    wsc(ve)
                    VE(ve.reciprocal(g8[:, 7:8], g8[:, 6:7]))
                    wv(ve)
                    VE(ve.tensor_scalar(g8[:, 16:24], g8[:, 16:24], g8[:, 7:8], None, ALU.mult))
                    wv(ve)
                    VE(ve.tensor_scalar(g8[:, 8:16], g8[:, 8:16], -1.0, 1.0, ALU.mult, ALU.add))
                    wv(ve)
                    VE(ve.tensor_tensor(wsel[:, (g * 4 + m) * 8:(g * 4 + m) * 8 + 8],
                                        g8[:, 16:24], g8[:, 8:16], ALU.mult))
                    wv(ve)
                wv(sy); wdg(sy)
                n = dmas(acc[:].rearrange("p (a c) -> p a c", a=4),
                         s_y[g * 512:(g + 1) * 512, :]
                         .rearrange("(a p) c -> p a c", p=128))
                ve.wait_ge(ds, n)
                with ExitStack() as ph2:
                    wge = ph2.enter_context(nc.sbuf_tensor(f"mo_we{g}", [128, NT * 512], F32))
                    wue = ph2.enter_context(nc.sbuf_tensor(f"mo_wu{g}", [128, NT * 512], F32))
                    he = ph2.enter_context(nc.sbuf_tensor(f"mo_he{g}", [128, 4 * 1024], F32))
                    heT = ph2.enter_context(nc.sbuf_tensor(f"mo_ht{g}", [128, 8 * 512], F32))
                    tmp = ph2.enter_context(nc.sbuf_tensor(f"mo_t{g}", [128, 512], F32))
                    for e in range(6):
                        wv(gp)
                        dmag(beB[:, 0:1024], Wd["bgate"][e:e + 1, :].partition_broadcast(128))
                        dmag(beB[:, 1024:2048], Wd["bup"][e:e + 1, :].partition_broadcast(128))
                        dmag(bdB[:], Wd["bdown"][e:e + 1, :].partition_broadcast(128))
                        for nch in range(2):
                            wt(sy)
                            ngs = dmas_split(wge,
                                             Wd["wgate"][e, :, nch * 512:(nch + 1) * 512],
                                             2048, 512)
                            nw = dmas(wue[:].rearrange("p (n c) -> p n c", n=16),
                                      Wd["wup"][e, :, nch * 512:(nch + 1) * 512]
                                      .rearrange("(n p) c -> p n c", p=128))
                            tn_wait_split(ngs)
                            tn.wait_ge(cv, cnt["cv"])
                            for m in range(4):
                                for k in range(NT):
                                    (TN if k == NT - 1 else TNq)(
                                        tn.matmul(PS[m][:, 0:512],
                                                  fm[:, k * 512 + m * 128: k * 512 + m * 128 + 128],
                                                  wge[:, k * 512:(k + 1) * 512],
                                                  start=(k == 0), stop=(k == NT - 1)))
                            wt(ve); wt(sc)
                            ve.wait_ge(dg, cnt["dg"])
                            sc.wait_ge(dg, cnt["dg"])
                            for m in range(4):
                                wv(ve)
                                VE(ve.tensor_add(tmp[:], PS[m][:, 0:512],
                                                 beB[:, nch * 512: nch * 512 + 512]))
                                wv(sc); wsc(sc)
                                SC(sc.activation(he[:, m * 1024 + nch * 512:
                                                    m * 1024 + nch * 512 + 512],
                                                 tmp[:], AF.Silu))
                            wsc(tn); wv(tn)
                            tn.wait_ge(ds, nw)
                            for m in range(4):
                                for k in range(NT):
                                    (TN if k == NT - 1 else TNq)(
                                        tn.matmul(PS[m][:, 0:512],
                                                  fm[:, k * 512 + m * 128: k * 512 + m * 128 + 128],
                                                  wue[:, k * 512:(k + 1) * 512],
                                                  start=(k == 0), stop=(k == NT - 1)))
                            wt(ve); wt(sc); wsc(ve)
                            for m in range(4):
                                wv(ve)
                                VE(ve.tensor_add(tmp[:], PS[m][:, 0:512],
                                                 beB[:, 1024 + nch * 512: 1024 + nch * 512 + 512]))
                                wv(ve)
                                VE(ve.tensor_tensor(he[:, m * 1024 + nch * 512:
                                                       m * 1024 + nch * 512 + 512],
                                                    he[:, m * 1024 + nch * 512:
                                                       m * 1024 + nch * 512 + 512],
                                                    tmp[:], ALU.mult))
                        for m in range(4):
                            for k2 in range(8):
                                wv(tn)
                                TN(tn.transpose(PS[5][:, 0:128],
                                                he[:, m * 1024 + k2 * 128: m * 1024 + k2 * 128 + 128],
                                                idt[:]))
                                wt(ve)
                                VE(ve.tensor_copy(heT[:, k2 * 512 + m * 128: k2 * 512 + m * 128 + 128],
                                                  PS[5][:, 0:128]))
                        dn_after = [cnt["ct"], cnt["ct"]]

                        def dload(nn_):
                            sy.wait_ge(ct, dn_after[nn_ % 2])
                            off = (nn_ % 2) * 4096
                            return dmas(wue[:, off: off + 4096]
                                        .rearrange("p (n c) -> p n c", n=8),
                                        Wd["wdown"][e, :, nn_ * 512:(nn_ + 1) * 512]
                                        .rearrange("(n p) c -> p n c", p=128))
                        nw_d = dload(0)
                        for nn in range(4):
                            half = nn % 2
                            nw_cur = nw_d
                            if nn < 3:
                                nw_d = dload(nn + 1)
                            tn.wait_ge(ds, nw_cur)
                            for m in range(4):
                                bank = m % 4
                                tn.wait_ge(cv, cnt["cv"])
                                wv(tn)
                                for k2 in range(8):
                                    (TN if k2 == 7 else TNq)(
                                        tn.matmul(PS[bank][:, 0:512],
                                                  heT[:, k2 * 512 + m * 128: k2 * 512 + m * 128 + 128],
                                                  wue[:, half * 4096 + k2 * 512:
                                                      half * 4096 + (k2 + 1) * 512],
                                                  start=(k2 == 0), stop=(k2 == 7)))
                                wt(ve); wdg(ve); wv(ve); wsc(ve)
                                VE(ve.tensor_add(tmp[:], PS[bank][:, 0:512],
                                                 bdB[:, nn * 512:(nn + 1) * 512]))
                                wv(ve)
                                VE(ve.scalar_tensor_tensor(
                                    acc[:, m * 2048 + nn * 512: m * 2048 + nn * 512 + 512],
                                    tmp[:], wsel[:, (g * 4 + m) * 8 + e:(g * 4 + m) * 8 + e + 1],
                                    acc[:, m * 2048 + nn * 512: m * 2048 + nn * 512 + 512],
                                    ALU.mult, ALU.add))
                            dn_after[half] = cnt["ct"]
                with ExitStack() as ph3:
                    wsu = ph3.enter_context(nc.sbuf_tensor(f"mo_su{g}", [128, NT * 512], F32))
                    sh = ph3.enter_context(nc.sbuf_tensor(f"mo_sh{g}", [128, 4 * 2048], F32))
                    shT = ph3.enter_context(nc.sbuf_tensor(f"mo_st{g}", [128, NT * 512], F32))
                    tmp = ph3.enter_context(nc.sbuf_tensor(f"mo_t2{g}", [128, 512], F32))
                    wv(gp)
                    dmag(beB[:], Wd["bs_up"][0:1, :].partition_broadcast(128))
                    dmag(bdB[:], Wd["bs_down"][0:1, :].partition_broadcast(128))
                    for nch in range(4):
                        wt(sy)
                        nws = dmas_split(wsu, Wd["ws_up"][:, nch * 512:(nch + 1) * 512],
                                         2048, 512)
                        tn_wait_split(nws)
                        for m in range(4):
                            bank = m % 4
                            tn.wait_ge(cv, cnt["cv"])
                            wv(tn)
                            for k in range(NT):
                                (TN if k == NT - 1 else TNq)(
                                    tn.matmul(PS[bank][:, 0:512],
                                              fm[:, k * 512 + m * 128: k * 512 + m * 128 + 128],
                                              wsu[:, k * 512:(k + 1) * 512],
                                              start=(k == 0), stop=(k == NT - 1)))
                            wt(ve); wdg(ve); wv(ve); wsc(ve)
                            VE(ve.tensor_add(tmp[:], PS[bank][:, 0:512],
                                             beB[:, nch * 512:(nch + 1) * 512]))
                            wv(sc); wsc(sc)
                            SC(sc.activation(sh[:, m * 2048 + nch * 512:
                                                m * 2048 + nch * 512 + 512],
                                             tmp[:], AF.Silu))
                    wsc(tn); wsc(ve)
                    for m in range(4):
                        for k in range(NT):
                            wv(tn)
                            TN(tn.transpose(PS[5][:, 0:128],
                                            sh[:, m * 2048 + k * 128: m * 2048 + k * 128 + 128],
                                            idt[:]))
                            wt(ve)
                            VE(ve.tensor_copy(shT[:, k * 512 + m * 128: k * 512 + m * 128 + 128],
                                              PS[5][:, 0:128]))
                    for nn in range(4):
                        wt(sy)
                        nws = dmas_split(wsu, Wd["ws_down"][:, nn * 512:(nn + 1) * 512],
                                         2048, 512)
                        tn_wait_split(nws)
                        for m in range(4):
                            bank = m % 4
                            tn.wait_ge(cv, cnt["cv"])
                            wv(tn)
                            for k in range(NT):
                                (TN if k == NT - 1 else TNq)(
                                    tn.matmul(PS[bank][:, 0:512],
                                              shT[:, k * 512 + m * 128: k * 512 + m * 128 + 128],
                                              wsu[:, k * 512:(k + 1) * 512],
                                              start=(k == 0), stop=(k == NT - 1)))
                            wt(ve); wdg(ve); wv(ve); wsc(ve)
                            VE(ve.tensor_add(tmp[:], PS[bank][:, 0:512],
                                             bdB[:, nn * 512:(nn + 1) * 512]))
                            wv(ve)
                            VE(ve.tensor_add(acc[:, m * 2048 + nn * 512:
                                                 m * 2048 + nn * 512 + 512],
                                             acc[:, m * 2048 + nn * 512:
                                                 m * 2048 + nn * 512 + 512],
                                             tmp[:]))
                wv(gp); wdg(gp)
                dmag(out_d[g * 512:(g + 1) * 512, :]
                     .rearrange("(a p) c -> p a c", p=128),
                     acc[:].rearrange("p (a c) -> p a c", a=4))
            wdg(gp); wds(gp)
    except _StopBuild:
        for e in (tn, ve, sc, gp, sy):
            wds(e); wdg(e); wt(e); wv(e); wsc(e)

    if DEBUG:
        for nm_, sb_, shp_ in [("qg", qgsum, [1, 2048]), ("al", alsb, [128, 512]),
                               ("os", osbd, [128, 2048]), ("fgv", fgv, [128, 16]),
                               ("fwo", fwo, [128, 16]), ("wsel", wsel, [128, 64])]:
            o_ = nc.dram_tensor("dbgs_" + nm_, shp_, F32, kind="ExternalOutput").ap()
            dmag(o_[:, :], sb_[:])
        dbg = {"x1": (s_x1, [T, D]), "qk": (s_qk, [T, 2048]),
               "kk": (s_kk, [T, 512]), "v": (s_v, [T, 512]),
               "u": (s_u, [T, 8192]), "att": (s_att, [OWN, 2048]),
               "y": (s_y, [OWN, D])}
        for e in (tn, ve, sc, gp, sy):
            wds(e); wdg(e); wt(e); wv(e); wsc(e)
        for name, (src_t, shp) in dbg.items():
            o = nc.dram_tensor("dbg_" + name, shp, F32, kind="ExternalOutput").ap()
            dmag(o[:, :], src_t[:, :])
        wdg(gp)

    def _replay(rec):
        def emit(e):
            for name, a, k, fi in rec.ops:
                inst = getattr(e, name)(*a, **k)
                for sem_, n_ in fi.incs:
                    inst.then_inc(sem_, n_)
        return emit

    blk.tensor(_replay(tn))
    blk.vector(_replay(ve))
    blk.scalar(_replay(sc))
    blk.gpsimd(_replay(gp))
    blk.sync(_replay(sy))

    ost.close()
    return nc


def _host_consts():
    inv = 1.0 / (10000.0 ** (np.arange(0, 64, 2, dtype=np.float64) / 64.0))
    pos = np.arange(T, dtype=np.float64)
    fr = pos[:, None] * inv[None, :]
    emb = np.concatenate([fr, fr], axis=1)
    cs = np.concatenate([np.cos(emb), np.sin(emb)], axis=1).astype(np.float32)
    ident = np.eye(128, dtype=np.float32)
    import ml_dtypes
    identb = ident.astype(ml_dtypes.bfloat16)
    ones = np.ones((128, 128), dtype=np.float32)
    return cs, ident, identb, ones


_NC_CACHE = [None]


def kernel(**inputs):
    cs, ident, identb, ones = _host_consts()
    if _NC_CACHE[0] is None:
        _NC_CACHE[0] = build_nc()
    nc = _NC_CACHE[0]
    x0 = np.ascontiguousarray(np.asarray(inputs["L0_attn_hidden_states"], np.float32))
    keymap = {
        "l0_wq": "L0_attn_Wq", "l0_wk": "L0_attn_Wk", "l0_wv": "L0_attn_Wv",
        "l0_wphi": "L0_attn_Wphi", "l0_wo": "L0_attn_Wo",
        "wgv": "L0_ffn_w_gv", "wout": "L0_ffn_w_out",
        "l1_wq": "L1_attn_Wq", "l1_wk": "L1_attn_Wk", "l1_wv": "L1_attn_Wv",
        "l1_wo": "L1_attn_Wo",
        "wg": "L1_ffn_Wg", "wgate": "L1_ffn_Wgate", "wup": "L1_ffn_Wup",
        "wdown": "L1_ffn_Wdown", "ws_up": "L1_ffn_Ws_up", "ws_down": "L1_ffn_Ws_down",
    }
    row1 = {
        "wnorm": "L0_ffn_w_norm", "l1_bq": "L1_attn_bq", "l1_bk": "L1_attn_bk",
        "l1_bv": "L1_attn_bv", "l1_bo": "L1_attn_bo", "bg": "L1_ffn_bg",
        "bs_up": "L1_ffn_bs_up", "bs_down": "L1_ffn_bs_down",
    }
    shared = {}
    for k, src in keymap.items():
        shared[k] = np.ascontiguousarray(np.asarray(inputs[src], np.float32))
    for k, src in row1.items():
        shared[k] = np.ascontiguousarray(
            np.asarray(inputs[src], np.float32).reshape(1, -1))
    for k, src in {"bgate": "L1_ffn_bgate", "bup": "L1_ffn_bup",
                   "bdown": "L1_ffn_bdown"}.items():
        shared[k] = np.ascontiguousarray(np.asarray(inputs[src], np.float32))
    shared["c_id"] = ident
    shared["c_idb"] = identb
    shared["c_ones"] = ones

    in_maps = []
    for c in range(8):
        b, h = c // 2, c % 2
        m = dict(shared)
        xb = x0[b]
        if h == 1:
            xb = np.concatenate([xb[1024:], xb[:1024]], axis=0)
            m["c_cs"] = np.ascontiguousarray(
                np.concatenate([cs[1024:], cs[:1024]], axis=0))
        else:
            m["c_cs"] = cs
        m["x0"] = np.ascontiguousarray(xb)
        in_maps.append(m)

    res = run_bass_kernel_spmd(nc, in_maps, core_ids=list(range(8)))
    out = np.empty((4, T, D), dtype=np.float32)
    for c in range(8):
        b, h = c // 2, c % 2
        out[b, h * 1024:(h + 1) * 1024] = res.results[c]["out"]
    return out


# version 11 (final)
# speedup vs baseline: 1.0251x; 1.0251x over previous
import numpy as np
import concourse.bass as bass
import concourse.mybir as mybir
from concourse.bass_utils import run_bass_kernel_spmd
from concourse.alu_op_type import AluOpType as ALU
from contextlib import ExitStack

F32 = mybir.dt.float32
BF16 = mybir.dt.bfloat16
AF = mybir.ActivationFunctionType
AX = mybir.AxisListType.X

D = 2048
T = 2048
NT = 16
OWN = 1024
SCALE1 = float(1.0 / np.sqrt(128.0))
EPS = 1e-5
DEBUG = False
PHASES = 99


class _StopBuild(Exception):
    pass


class _FakeInst:
    __slots__ = ("incs",)

    def __init__(self):
        self.incs = []

    def then_inc(self, sem, n):
        self.incs.append((sem, n))
        return self


class _EngRec:

    def __init__(self):
        self.ops = []

    def __getattr__(self, name):
        def w(*a, **k):
            fi = _FakeInst()
            self.ops.append((name, a, k, fi))
            return fi
        return w


def build_nc():
    nc = bass.Bass("TRN2", target_bir_lowering=False, debug=False)
    x0_d = nc.dram_tensor("x0", [T, D], F32, kind="ExternalInput").ap()
    ccs_d = nc.dram_tensor("c_cs", [T, 128], F32, kind="ExternalInput").ap()
    cid_d = nc.dram_tensor("c_id", [128, 128], F32, kind="ExternalInput").ap()
    cidb_d = nc.dram_tensor("c_idb", [128, 128], BF16, kind="ExternalInput").ap()
    cones_d = nc.dram_tensor("c_ones", [128, 128], F32, kind="ExternalInput").ap()

    def din(name, shape):
        return nc.dram_tensor(name, shape, F32, kind="ExternalInput").ap()

    Wd = {
        "l0_wq": din("l0_wq", [D, 2048]), "l0_wk": din("l0_wk", [D, 512]),
        "l0_wv": din("l0_wv", [D, 512]), "l0_wphi": din("l0_wphi", [D, 2048]),
        "l0_wo": din("l0_wo", [D, 2048]), "wnorm": din("wnorm", [1, D]),
        "wgv": din("wgv", [16384, D]), "wout": din("wout", [D, 8192]),
        "l1_wq": din("l1_wq", [D, 2048]), "l1_bq": din("l1_bq", [1, 2048]),
        "l1_wk": din("l1_wk", [D, 256]), "l1_bk": din("l1_bk", [1, 256]),
        "l1_wv": din("l1_wv", [D, 256]), "l1_bv": din("l1_bv", [1, 256]),
        "l1_wo": din("l1_wo", [D, 2048]), "l1_bo": din("l1_bo", [1, 2048]),
        "wg": din("wg", [D, 8]), "bg": din("bg", [1, 8]),
        "wgate": din("wgate", [6, D, 1024]), "bgate": din("bgate", [6, 1024]),
        "wup": din("wup", [6, D, 1024]), "bup": din("bup", [6, 1024]),
        "wdown": din("wdown", [6, 1024, D]), "bdown": din("bdown", [6, D]),
        "ws_up": din("ws_up", [D, 2048]), "bs_up": din("bs_up", [1, 2048]),
        "ws_down": din("ws_down", [2048, D]), "bs_down": din("bs_down", [1, D]),
    }
    out_d = nc.dram_tensor("out", [OWN, D], F32, kind="ExternalOutput").ap()

    s_x1 = nc.dram_tensor("s_x1", [T, D], F32, kind="Internal").ap()
    s_qk = nc.dram_tensor("s_qk", [T, 2048], F32, kind="Internal").ap()
    s_kk = nc.dram_tensor("s_kk", [T, 512], F32, kind="Internal").ap()
    s_v = nc.dram_tensor("s_v", [T, 512], F32, kind="Internal").ap()
    s_u = nc.dram_tensor("s_u", [T, 8192], F32, kind="Internal").ap()
    s_wo = nc.dram_tensor("s_wo", [4, 64, 128, 512], BF16, kind="Internal").ap()
    s_att = nc.dram_tensor("s_att", [OWN, 2048], F32, kind="Internal").ap()
    s_y = nc.dram_tensor("s_y", [OWN, D], F32, kind="Internal").ap()

    ost = ExitStack(); ost.__enter__()
    ds = ost.enter_context(nc.semaphore("ds"))
    dg = ost.enter_context(nc.semaphore("dg"))
    ct = ost.enter_context(nc.semaphore("ct"))
    cv = ost.enter_context(nc.semaphore("cv"))
    cs = ost.enter_context(nc.semaphore("cs"))
    cg = ost.enter_context(nc.semaphore("cg"))
    cnt = {"ds": 0, "dg": 0, "ct": 0, "cv": 0, "cs": 0, "cg": 0}

    blk = ost.enter_context(nc.Block())
    eng = {}

    @blk.tensor
    def _(e): eng["t"] = e

    @blk.vector
    def _(e): eng["v"] = e

    @blk.scalar
    def _(e): eng["s"] = e

    @blk.gpsimd
    def _(e): eng["g"] = e

    @blk.sync
    def _(e): eng["y"] = e

    rtn, rve, rsc, rgp, rsy = eng["t"], eng["v"], eng["s"], eng["g"], eng["y"]
    tn, ve, sc, gp, sy = _EngRec(), _EngRec(), _EngRec(), _EngRec(), _EngRec()

    def TN(inst):
        inst.then_inc(ct, 1); cnt["ct"] += 1; return inst

    def TNq(inst):
        return inst

    def VE(inst):
        inst.then_inc(cv, 1); cnt["cv"] += 1; return inst

    def SC(inst):
        inst.then_inc(cs, 1); cnt["cs"] += 1; return inst

    def dmas(out, in_, slow=False):
        sy.dma_start(out=out, in_=in_, allow_slow_non_contiguous=slow).then_inc(ds, 16)
        cnt["ds"] += 16
        return cnt["ds"]

    def dmag(out, in_):
        gp.dma_start(out=out, in_=in_).then_inc(dg, 16)
        cnt["dg"] += 16
        return cnt["dg"]

    def dmas_split(buf, src, rows, ncol, parts=4):
        ns_ = []
        step = rows // 128 // parts
        for q in range(parts):
            ns_.append(dmas(
                buf[:, q * step * ncol:(q + 1) * step * ncol]
                .rearrange("p (n c) -> p n c", n=step),
                src[q * step * 128:(q + 1) * step * 128, :]
                .rearrange("(n p) c -> p n c", p=128)))
        return ns_

    def tn_wait_split(ns_):
        for n_ in ns_:
            tn.wait_ge(ds, n_); heatmm()

    def wt(e): e.wait_ge(ct, cnt["ct"])

    def wv(e): e.wait_ge(cv, cnt["cv"])

    def wsc(e): e.wait_ge(cs, cnt["cs"])

    def wds(e): e.wait_ge(ds, cnt["ds"])

    def wdg(e): e.wait_ge(dg, cnt["dg"])

    def fence():
        for e in (tn, ve, sc, gp, sy):
            wds(e); wdg(e); wt(e); wv(e); wsc(e)

    def sball(name, shape, dt=F32):
        return ost.enter_context(nc.sbuf_tensor(name, shape, dt))

    idt = sball("idt", [128, 128])
    idb = sball("idb", [128, 128], BF16)
    onest = sball("onest", [128, 128])
    mu0 = sball("mu0", [128, NT]); rs0 = sball("rs0", [128, NT])
    mu1 = sball("mu1", [128, NT]); rs1 = sball("rs1", [128, NT])
    epst = sball("epst", [128, 1])
    qgsum = sball("qgsum", [1, 2048])
    osbd = sball("osbd", [128, NT * 128])
    fgv = sball("fgv", [128, NT])
    fwo = sball("fwo", [128, NT])
    umax = sball("umax", [128, NT])
    mgv = sball("mgv", [128, 1]); thrgv = sball("thrgv", [128, 1])
    mwo = sball("mwo", [128, 1]); thrwo = sball("thrwo", [128, 1])
    wsel = sball("wsel", [128, 64])
    bqc = sball("bqc", [128, 16])
    bgB = sball("bgB", [128, 8])

    PS = [ost.enter_context(nc.psum_tensor(f"ps{i}", [128, 512], F32))
          for i in range(7)]
    PSB = ost.enter_context(nc.psum_tensor("psb", [128, 1024], BF16))

    def heatmm():
        TNq(tn.matmul(PS[5][0:1, 508:509], idt[:, 0:1], idt[:, 0:1],
                      start=False, stop=True))

    def heatv():
        tn.wait_ge(cv, cnt["cv"]); heatmm()

    def heats():
        tn.wait_ge(cs, cnt["cs"]); heatmm()

    dmas(idt[:], cid_d[:, :])
    dmas(idb[:], cidb_d[:, :])
    dmas(onest[:], cones_d[:, :])
    dmas(bqc[:], Wd["l1_bq"][0, :].rearrange("(m p) -> p m", p=128), slow=True)
    dmag(bgB[:], Wd["bg"][0:1, :].partition_broadcast(128))
    wds(ve); wdg(ve)
    VE(ve.memset(epst[:], EPS))
    VE(ve.memset(qgsum[:], 0.0))
    VE(ve.memset(umax[:], 0.0))
    VE(ve.memset(osbd[:], 0.0))

    def ln_stats(xt, mu_col, rs_col, scr, sq):
        wv(ve)
        VE(ve.reduce_sum(scr[:, 0:1], xt, axis=AX))
        heatv()
        wsc(sc); wv(sc)
        SC(sc.activation(sq, xt, AF.Square, accum_out=scr[:, 1:2]))
        heats()
        wv(ve); wsc(ve)
        VE(ve.tensor_scalar(mu_col, scr[:, 0:1], 1.0 / D, None, ALU.mult))
        VE(ve.tensor_scalar(scr[:, 2:3], scr[:, 1:2], 1.0 / D, None, ALU.mult))
        wv(ve)
        VE(ve.tensor_tensor(scr[:, 3:4], mu_col, mu_col, ALU.mult))
        wv(ve)
        VE(ve.tensor_sub(scr[:, 2:3], scr[:, 2:3], scr[:, 3:4]))
        wv(sc)
        SC(sc.activation(scr[:, 4:5], scr[:, 2:3], AF.Sqrt, bias=epst[:, 0:1]))
        wsc(ve)
        VE(ve.reciprocal(rs_col, scr[:, 4:5]))
        wv(ve)

    def apply_ln(dst, xt, mu_col, rs_col, scr):
        wsc(ve)
        VE(ve.tensor_scalar(scr, xt, mu_col, None, ALU.subtract))
        wv(ve); heatv()
        VE(ve.tensor_scalar(dst, scr, rs_col, None, ALU.mult))
        wv(ve); heatv()

    def build_fm(slab, fm, xtile, ntile, scr, mu, rs, src_d):
        for tl in range(4):
            t = slab * 4 + tl
            wv(sy); wsc(sy); wt(sy)
            n = dmas(xtile[:], src_d[t * 128:(t + 1) * 128, :])
            ve.wait_ge(ds, n)
            apply_ln(ntile[:], xtile[:], mu[:, t:t + 1], rs[:, t:t + 1], scr[:])
            for k in range(NT):
                wv(tn)
                TN(tn.transpose(PS[5][:, 0:128], ntile[:, k * 128:(k + 1) * 128], idt[:]))
                wt(ve)
                VE(ve.tensor_copy(fm[:, k * 512 + tl * 128: k * 512 + tl * 128 + 128],
                                  PS[5][:, 0:128]))
        wv(tn)

    def binary_round(dst, y, z, r, tb):
        wsc(ve)
        VE(ve.tensor_scalar(z, y, 128.5, None, ALU.add))
        wv(ve); heatv()
        VE(ve.tensor_scalar(tb, z, 128.0, 128.0, ALU.is_ge, ALU.mult))
        wv(ve)
        VE(ve.tensor_sub(r, z, tb))
        heatv()
        for b in (64.0, 32.0, 16.0, 8.0, 4.0, 2.0, 1.0):
            wv(ve)
            VE(ve.tensor_scalar(tb, r, b, b, ALU.is_ge, ALU.mult))
            wv(ve); heatv()
            VE(ve.tensor_sub(r, r, tb))
        wv(ve); heatv()
        VE(ve.tensor_sub(tb, z, r))
        wv(ve)
        VE(ve.tensor_scalar(dst, tb, -128.0, None, ALU.add))
        wv(ve); heatv()

    def _ckpt(k):
        if k >= PHASES:
            raise _StopBuild()

    try:
        _ckpt(0)
        with ExitStack() as ph:
            xt = ph.enter_context(nc.sbuf_tensor("s0x", [128, D], F32))
            xn = ph.enter_context(nc.sbuf_tensor("s0n", [128, D], F32))
            sq = ph.enter_context(nc.sbuf_tensor("s0q", [128, D], F32))
            scr = ph.enter_context(nc.sbuf_tensor("s0s", [128, 8], F32))
            for t in range(NT):
                wv(sy); wsc(sy)
                n = dmas(xt[:], x0_d[t * 128:(t + 1) * 128, :])
                ve.wait_ge(ds, n); sc.wait_ge(ds, n)
                ln_stats(xt[:], mu0[:, t:t + 1], rs0[:, t:t + 1], scr, sq[:])
                wdg(ve)
                apply_ln(xn[:], xt[:], mu0[:, t:t + 1], rs0[:, t:t + 1], sq[:])
                VE(ve.tensor_add(xn[:], xn[:], xt[:]))
                wv(gp)
                dmag(s_x1[t * 128:(t + 1) * 128, :], xn[:])
            fence()

        _ckpt(1)
        def rope_kappa(csb, dst, q, t, t1, t2):
            nh = 8
            cos = csb[:, t * 128: t * 128 + 64].unsqueeze(1).broadcast_to((128, nh, 64))
            sin1 = csb[:, t * 128 + 64: t * 128 + 96].unsqueeze(1).broadcast_to((128, nh, 32))
            sin2 = csb[:, t * 128 + 96: t * 128 + 128].unsqueeze(1).broadcast_to((128, nh, 32))
            q3 = q.rearrange("p (h d) -> p h d", h=nh)
            wv(ve)
            VE(ve.tensor_tensor(t1.rearrange("p (h d) -> p h d", h=nh), q3, cos, ALU.mult))
            VE(ve.scalar_tensor_tensor(
                t2.rearrange("p (h d) -> p h d", h=nh)[:, :, 0:32],
                q3[:, :, 32:64], -1.0, sin1, ALU.mult, ALU.mult))
            VE(ve.tensor_tensor(
                t2.rearrange("p (h d) -> p h d", h=nh)[:, :, 32:64],
                q3[:, :, 0:32], sin2, ALU.mult))
            wv(ve); heatv()
            VE(ve.tensor_add(t1, t1, t2))
            wv(ve)
            VE(ve.tensor_scalar(t2, t1, 0.0, None, ALU.min))
            wv(sc); wsc(sc)
            SC(sc.activation(dst, t2, AF.Exp))
            heats()
            wsc(ve)
            VE(ve.tensor_scalar(t2, t1, 0.0, None, ALU.max))
            wv(ve)
            VE(ve.tensor_add(dst, dst, t2))
            wv(ve); heatv()

        with ExitStack() as ph:
            fm = ph.enter_context(nc.sbuf_tensor("pa_fm", [128, NT * 512], F32))
            wsl = ph.enter_context(nc.sbuf_tensor("pa_w", [128, NT * 1024], F32))
            xtile = ph.enter_context(nc.sbuf_tensor("pa_x", [128, D], F32))
            ntile = ph.enter_context(nc.sbuf_tensor("pa_n", [128, D], F32))
            scr = ph.enter_context(nc.sbuf_tensor("pa_s", [128, D], F32))
            qt = ph.enter_context(nc.sbuf_tensor("pa_q", [128, 512], F32))
            kt = ph.enter_context(nc.sbuf_tensor("pa_k", [128, 512], F32))
            t1 = ph.enter_context(nc.sbuf_tensor("pa_1", [128, 512], F32))
            t2 = ph.enter_context(nc.sbuf_tensor("pa_2", [128, 512], F32))
            csb = ph.enter_context(nc.sbuf_tensor("pa_cs", [128, NT * 128], F32))
            dmas(csb[:].rearrange("p (n c) -> p n c", n=NT),
                 ccs_d.rearrange("(n p) c -> p n c", p=128))

            for pas in ("q01", "q23", "kv"):
                wt(sy)
                if pas == "q01":
                    nws = dmas_split(wsl, Wd["l0_wq"][:, 0:1024], 2048, 1024, parts=8)
                elif pas == "q23":
                    nws = dmas_split(wsl, Wd["l0_wq"][:, 1024:2048], 2048, 1024, parts=8)
                else:
                    nws = []
                    for q_ in range(4):
                        dmas(wsl[:].rearrange("p (n x c) -> p n x c", n=16, x=2)
                             [:, q_ * 4:(q_ + 1) * 4, 0, :],
                             Wd["l0_wk"][q_ * 512:(q_ + 1) * 512, :]
                             .rearrange("(n p) c -> p n c", p=128))
                        nws.append(dmas(
                            wsl[:].rearrange("p (n x c) -> p n x c", n=16, x=2)
                            [:, q_ * 4:(q_ + 1) * 4, 1, :],
                            Wd["l0_wv"][q_ * 512:(q_ + 1) * 512, :]
                            .rearrange("(n p) c -> p n c", p=128)))
                tn_wait_split(nws)
                for slab in range(4):
                    wt(ve)
                    build_fm(slab, fm, xtile, ntile, scr, mu0, rs0, x0_d)
                    evac_at = [0, 0, 0, 0]
                    for tl in range(4):
                        t = slab * 4 + tl
                        for nch in range(2):
                            bank = (tl * 2 + nch) % 4
                            tn.wait_ge(cv, evac_at[bank])
                            for k in range(NT):
                                (TN if k == NT - 1 else TNq)(
                                    tn.matmul(PS[bank][:, 0:512],
                                              fm[:, k * 512 + tl * 128: k * 512 + tl * 128 + 128],
                                              wsl[:, k * 1024 + nch * 512: k * 1024 + nch * 512 + 512],
                                              start=(k == 0), stop=(k == NT - 1)))
                            ve.wait_ge(ct, cnt["ct"])
                            ve.wait_ge(dg, cnt["dg"])
                            VE(ve.tensor_copy(qt[:], PS[bank][:, 0:512]))
                            evac_at[bank] = cnt["cv"]
                            if pas == "kv" and nch == 1:
                                wv(gp)
                                dmag(s_v[t * 128:(t + 1) * 128, :], qt[:])
                                continue
                            sc.wait_ge(dg, cnt["dg"])
                            ve.wait_ge(dg, cnt["dg"])
                            rope_kappa(csb, kt[:], qt[:], t, t1[:], t2[:])
                            if pas == "kv":
                                wv(gp)
                                dmag(s_kk[t * 128:(t + 1) * 128, :], kt[:])
                            else:
                                off = (0 if pas == "q01" else 1024) + nch * 512
                                wv(tn)
                                TN(tn.matmul(PS[6][0:1, 0:512], onest[:, 0:1], kt[:],
                                             start=True, stop=True))
                                wt(ve)
                                VE(ve.tensor_add(qgsum[0:1, off:off + 512],
                                                 qgsum[0:1, off:off + 512],
                                                 PS[6][0:1, 0:512]))
                                wv(gp)
                                dmag(s_qk[t * 128:(t + 1) * 128, off:off + 512], kt[:])
            fence()

        _ckpt(2)
        with ExitStack() as ph:
            kka = ph.enter_context(nc.sbuf_tensor("pb_kka", [128, NT * 512], F32))
            vta = ph.enter_context(nc.sbuf_tensor("pb_vta", [128, NT * 512], F32))
            prod = ph.enter_context(nc.sbuf_tensor("pb_p", [128, 2048], F32))
            lg = ph.enter_context(nc.sbuf_tensor("pb_l", [128, NT * 32], F32))
            lgT = ph.enter_context(nc.sbuf_tensor("pb_lt", [32, 2048], F32))
            sm = ph.enter_context(nc.sbuf_tensor("pb_sm", [32, 8], F32))
            ex = ph.enter_context(nc.sbuf_tensor("pb_ex", [32, 2048], F32))
            ak2 = ph.enter_context(nc.sbuf_tensor("pb_ak", [128, 512], F32))
            qgB = ph.enter_context(nc.sbuf_tensor("pb_qg", [128, 2048], F32))
            alsb = ph.enter_context(nc.sbuf_tensor("pb_al", [128, NT * 32], F32))

            wv(sy); wt(sy)
            dmas(kka[:].rearrange("p (n c) -> p n c", n=NT),
                 s_kk.rearrange("(n p) c -> p n c", p=128))
            nkv = dmas(vta[:].rearrange("p (n c) -> p n c", n=NT),
                       s_v.rearrange("(n p) c -> p n c", p=128))
            ve.wait_ge(ds, nkv)
            tn.wait_ge(ds, nkv)

            VE(ve.tensor_scalar(qgsum[:], qgsum[:], 1.0 / T, None, ALU.mult))
            wv(tn)
            for j in range(4):
                TN(tn.matmul(PS[j][0:128, 0:512], onest[0:1, 0:128],
                             qgsum[0:1, j * 512:(j + 1) * 512], start=True, stop=True))
            wt(ve)
            for j in range(4):
                VE(ve.tensor_copy(qgB[:, j * 512:(j + 1) * 512], PS[j][0:128, 0:512]))
            wv(ve)
            qg3 = qgB[:].rearrange("p (a c d) -> p a c d", a=8, c=4)
            for t in range(NT):
                kk4 = kka[:, t * 512:(t + 1) * 512].rearrange("p (h d) -> p h d", h=8) \
                    .unsqueeze(2).broadcast_to((128, 8, 4, 64))
                wv(ve)
                VE(ve.tensor_tensor(prod[:].rearrange("p (a c d) -> p a c d", a=8, c=4),
                                    kk4, qg3, ALU.mult))
                wv(ve); heatv()
                VE(ve.reduce_sum(lg[:, t * 32:(t + 1) * 32].rearrange("p (a c) -> p a c", a=8),
                                 prod[:].rearrange("p (a c d) -> p a c d", a=8, c=4),
                                 axis=AX))
                wv(tn)
                TN(tn.transpose(PS[5][0:32, 0:128], lg[:, t * 32:(t + 1) * 32], idt[:]))
                wt(ve)
                VE(ve.tensor_copy(lgT[:, t * 128:(t + 1) * 128], PS[5][0:32, 0:128]))
            wv(ve)
            VE(ve.reduce_max(sm[:, 0:1], lgT[:], axis=AX))
            wv(ve); heatv()
            VE(ve.tensor_scalar(sm[:, 1:2], sm[:, 0:1], -1.0, None, ALU.mult))
            wv(sc); wsc(sc)
            SC(sc.activation(ex[:], lgT[:], AF.Exp, bias=sm[:, 1:2], accum_out=sm[:, 2:3]))
            heats()
            wsc(ve)
            VE(ve.reciprocal(sm[:, 3:4], sm[:, 2:3]))
            wv(ve)
            VE(ve.tensor_scalar(sm[:, 4:5], sm[:, 3:4], float(T), None, ALU.mult))
            wv(ve)
            VE(ve.tensor_scalar(ex[:], ex[:], sm[:, 4:5], None, ALU.mult))
            wv(ve); heatv()
            for t in range(NT):
                wv(tn)
                TN(tn.transpose(PS[5][0:128, 0:32], ex[:, t * 128:(t + 1) * 128],
                                idt[0:32, 0:32]))
                wt(ve)
                VE(ve.tensor_copy(alsb[:, t * 32:(t + 1) * 32], PS[5][0:128, 0:32]))
            ev_bank = [cnt["cv"]] * 4
            last_half = [0, 0]
            for s in range(8):
                b0 = (s % 2) * 2
                for t in range(NT):
                    half = t % 2
                    ve.wait_ge(ct, last_half[half])
                    kkb = kka[:, t * 512 + s * 64: t * 512 + s * 64 + 64] \
                        .unsqueeze(1).broadcast_to((128, 4, 64))
                    alb = alsb[:, t * 32 + s * 4: t * 32 + s * 4 + 4] \
                        .unsqueeze(2).broadcast_to((128, 4, 64))
                    wv(ve)
                    VE(ve.tensor_tensor(ak2[:, half * 256: half * 256 + 256]
                                        .rearrange("p (c d) -> p c d", c=4),
                                        kkb, alb, ALU.mult))
                    wv(tn)
                    if t == 0:
                        tn.wait_ge(cv, ev_bank[b0])
                        tn.wait_ge(cv, ev_bank[b0 + 1])
                    for c in range(2):
                        (TN if c == 1 else TNq)(
                            tn.matmul(PS[b0 + c][0:128, 0:64],
                                      ak2[:, half * 256 + c * 128:
                                          half * 256 + c * 128 + 128],
                                      vta[:, t * 512 + s * 64: t * 512 + s * 64 + 64],
                                      start=(t == 0), stop=(t == NT - 1)))
                    last_half[half] = cnt["ct"]
                wt(ve)
                for c in range(2):
                    r = 2 * s + c
                    VE(ve.tensor_copy(osbd[0:64, r * 128: r * 128 + 64],
                                      PS[b0 + c][0:64, 0:64]))
                    VE(ve.tensor_copy(osbd[64:128, r * 128 + 64: r * 128 + 128],
                                      PS[b0 + c][64:128, 0:64]))
                ev_bank[b0] = cnt["cv"]
                ev_bank[b0 + 1] = cnt["cv"]
            fence()

        _ckpt(3)
        with ExitStack() as ph:
            qkT = ph.enter_context(nc.sbuf_tensor("pc_qt", [128, NT * 512], F32))
            res = ph.enter_context(nc.sbuf_tensor("pc_rs", [128, NT * 512], F32))
            xtile = ph.enter_context(nc.sbuf_tensor("pc_x", [128, D], F32))
            ntile = ph.enter_context(nc.sbuf_tensor("pc_n", [128, D], F32))
            scr = ph.enter_context(nc.sbuf_tensor("pc_s", [128, D], F32))
            wslab = ph.enter_context(nc.sbuf_tensor("pc_w", [128, NT * 512], F32))
            x1rw = ph.enter_context(nc.sbuf_tensor("pc_r", [128, 4 * 2048], F32))
            for slab in range(4):
                for tl in range(4):
                    t = slab * 4 + tl
                    wt(sy)
                    n = dmas(xtile[:], s_qk[t * 128:(t + 1) * 128, :])
                    tn.wait_ge(ds, n)
                    for k in range(NT):
                        wv(tn)
                        TN(tn.transpose(PS[5][:, 0:128], xtile[:, k * 128:(k + 1) * 128], idt[:]))
                        wt(ve)
                        VE(ve.tensor_copy(qkT[:, k * 512 + tl * 128: k * 512 + tl * 128 + 128],
                                          PS[5][:, 0:128]))
                ev = [0, 0, 0, 0]
                for r in range(NT):
                    bank = r % 4
                    tn.wait_ge(cv, ev[bank])
                    wv(tn)
                    TN(tn.matmul(PS[bank][:, 0:512], osbd[:, r * 128:(r + 1) * 128],
                                 qkT[:, r * 512:(r + 1) * 512], start=True, stop=True))
                    wt(ve)
                    VE(ve.tensor_copy(res[:, r * 512:(r + 1) * 512], PS[bank][:, 0:512]))
                    ev[bank] = cnt["cv"]
                wt(ve); wv(tn)
                build_fm(slab, qkT, xtile, ntile, scr, mu0, rs0, x0_d)
                for fs in range(4):
                    wt(sy)
                    nws = dmas_split(wslab, Wd["l0_wphi"][:, fs * 512:(fs + 1) * 512],
                                     2048, 512)
                    tn_wait_split(nws)
                    for fl in range(4):
                        f = fs * 4 + fl
                        bank = fl % 4
                        tn.wait_ge(cv, cnt["cv"])
                        for k in range(NT):
                            (TN if k == NT - 1 else TNq)(
                                tn.matmul(PS[bank][:, 0:512],
                                          wslab[:, k * 512 + fl * 128: k * 512 + fl * 128 + 128],
                                          qkT[:, k * 512:(k + 1) * 512],
                                          start=(k == 0), stop=(k == NT - 1)))
                        wt(ve)
                        VE(ve.tensor_tensor(res[:, f * 512:(f + 1) * 512],
                                            PS[bank][:, 0:512],
                                            res[:, f * 512:(f + 1) * 512], ALU.mult))
                wv(sy); wdg(sy)
                n = dmas(x1rw[:].rearrange("p (a c) -> p a c", a=4),
                         s_x1[slab * 512:(slab + 1) * 512, :]
                         .rearrange("(a p) c -> p a c", p=128))
                ve.wait_ge(ds, n)
                for nn in range(4):
                    wt(sy)
                    nws = dmas_split(wslab, Wd["l0_wo"][:, nn * 512:(nn + 1) * 512],
                                     2048, 512)
                    tn_wait_split(nws)
                    for m in range(4):
                        bank = m % 4
                        tn.wait_ge(cv, cnt["cv"])
                        wv(tn)
                        for k in range(NT):
                            (TN if k == NT - 1 else TNq)(
                                tn.matmul(PS[bank][:, 0:512],
                                          res[:, k * 512 + m * 128: k * 512 + m * 128 + 128],
                                          wslab[:, k * 512:(k + 1) * 512],
                                          start=(k == 0), stop=(k == NT - 1)))
                        wt(ve)
                        VE(ve.tensor_add(x1rw[:, m * 2048 + nn * 512: m * 2048 + nn * 512 + 512],
                                         x1rw[:, m * 2048 + nn * 512: m * 2048 + nn * 512 + 512],
                                         PS[bank][:, 0:512]))
                wv(gp); wdg(gp)
                dmag(s_x1[slab * 512:(slab + 1) * 512, :]
                     .rearrange("(a p) c -> p a c", p=128),
                     x1rw[:].rearrange("p (a c) -> p a c", a=4))
                gp.wait_ge(dg, cnt["dg"])
            fence()

        _ckpt(4)
        with ExitStack() as ph:
            wch = ph.enter_context(nc.sbuf_tensor("fw_w", [128, 2 * 8192], F32))
            ac = ph.enter_context(nc.sbuf_tensor("fw_a", [128, 4], F32))
            tot = ph.enter_context(nc.sbuf_tensor("fw_t", [1, 2], F32))
            VE(ve.memset(ac[:], 0.0))
            chunks = [("gv", c) for c in range(0, 128, 4)] + \
                     [("wo", c) for c in range(NT)]
            ld_at = [0, 0]

            def _fw_load(i):
                kind, c = chunks[i]
                half = i % 2
                dst = wch[:, half * 8192:(half + 1) * 8192]
                (sy if half == 0 else gp).wait_ge(cv, ld_at[half])
                do = dmas if half == 0 else dmag
                if kind == "gv":
                    return do(dst.rearrange("p (a c) -> p a c", a=4),
                              Wd["wgv"][c * 128:(c + 4) * 128, :]
                              .rearrange("(a p) c -> p a c", p=128))
                return do(dst, Wd["wout"][c * 128:(c + 1) * 128, :])
            pend = [_fw_load(0), _fw_load(1)]
            for i in range(len(chunks)):
                half = i % 2
                kind, c = chunks[i]
                ve.wait_ge(ds if half == 0 else dg, pend[half])
                col, accc = (1, 0) if kind == "gv" else (3, 2)
                for q in range(4):
                    wv(ve)
                    VE(ve.tensor_reduce(ac[:, col:col + 1],
                                        wch[:, half * 8192 + q * 2048:
                                            half * 8192 + (q + 1) * 2048],
                                        AX, ALU.add, apply_absolute_value=True))
                    wv(ve)
                    VE(ve.tensor_add(ac[:, accc:accc + 1], ac[:, accc:accc + 1],
                                     ac[:, col:col + 1]))
                    heatv()
                ld_at[half] = cnt["cv"]
                if i + 2 < len(chunks):
                    pend[half] = _fw_load(i + 2)
            wv(tn)
            TN(tn.matmul(PS[6][0:1, 0:2], onest[:, 0:1],
                         ac[:].rearrange("p (a c) -> p a c", a=2)[:, :, 0:1]
                         .squeeze(2), start=True, stop=True))
            wt(ve)
            VE(ve.tensor_copy(tot[0:1, 0:2], PS[6][0:1, 0:2]))
            wv(tn)
            TN(tn.matmul(PS[5][0:128, 0:2], onest[0:1, 0:128], tot[0:1, 0:2],
                         start=True, stop=True))
            wt(ve)
            VE(ve.tensor_scalar(mgv[:], PS[5][0:128, 0:1], 1.0 / (16384.0 * 2048.0),
                                None, ALU.mult))
            VE(ve.tensor_scalar(mwo[:], PS[5][0:128, 1:2], 1.0 / (2048.0 * 8192.0),
                                None, ALU.mult))
            wv(ve)
            VE(ve.tensor_scalar(thrgv[:], mgv[:], 0.5, None, ALU.mult))
            VE(ve.tensor_scalar(thrwo[:], mwo[:], 0.5, None, ALU.mult))
            VE(ve.tensor_scalar(mgv[:], mgv[:], EPS, None, ALU.max))
            wv(ve)
            VE(ve.tensor_scalar(mwo[:], mwo[:], EPS, None, ALU.max))
            wv(ve)
            fence()

        _ckpt(5)
        with ExitStack() as ph:
            hqT = ph.enter_context(nc.sbuf_tensor("fh_t", [128, NT * 2048], BF16))
            with ExitStack() as ph1:
                xt = ph1.enter_context(nc.sbuf_tensor("fh_x", [128, D], F32))
                ht = ph1.enter_context(nc.sbuf_tensor("fh_h", [128, D], F32))
                hq = ph1.enter_context(nc.sbuf_tensor("fh_q", [128, 2 * D], BF16))
                z = ph1.enter_context(nc.sbuf_tensor("fh_z", [128, D], F32))
                r = ph1.enter_context(nc.sbuf_tensor("fh_r", [128, D], F32))
                tb = ph1.enter_context(nc.sbuf_tensor("fh_b", [128, D], F32))
                sq = ph1.enter_context(nc.sbuf_tensor("fh_sq", [128, D], F32))
                s8 = ph1.enter_context(nc.sbuf_tensor("fh_s", [128, 8], F32))
                wnormB = ph1.enter_context(nc.sbuf_tensor("fh_wn", [128, D], F32))
                dmag(wnormB[:], Wd["wnorm"][0:1, :].partition_broadcast(128))
                wdg(ve)
                fh_done = [0, 0]
                for t in range(NT):
                    wv(sy); wsc(sy)
                    n = dmas(xt[:], x0_d[t * 128:(t + 1) * 128, :])
                    ve.wait_ge(ds, n); sc.wait_ge(ds, n)
                    apply_ln(ht[:], xt[:], mu0[:, t:t + 1], rs0[:, t:t + 1], sq[:])
                    wv(sc); wsc(sc)
                    SC(sc.activation(sq[:], ht[:], AF.Square, accum_out=s8[:, 0:1]))
                    heats()
                    wsc(ve)
                    VE(ve.tensor_scalar(s8[:, 1:2], s8[:, 0:1], 1.0 / D, None, ALU.mult))
                    wv(sc)
                    SC(sc.activation(s8[:, 2:3], s8[:, 1:2], AF.Sqrt, bias=epst[:, 0:1]))
                    wsc(ve)
                    VE(ve.reciprocal(s8[:, 3:4], s8[:, 2:3]))
                    wv(ve)
                    VE(ve.tensor_scalar(ht[:], ht[:], s8[:, 3:4], None, ALU.mult))
                    wv(ve); heatv()
                    VE(ve.tensor_tensor(ht[:], ht[:], wnormB[:], ALU.mult))
                    wv(ve)
                    VE(ve.reduce_max(s8[:, 4:5], ht[:], axis=AX, apply_absolute_value=True))
                    wv(ve); heatv()
                    VE(ve.tensor_scalar(s8[:, 4:5], s8[:, 4:5], EPS, None, ALU.max))
                    wv(ve)
                    VE(ve.reciprocal(s8[:, 5:6], s8[:, 4:5]))
                    wv(ve)
                    VE(ve.tensor_scalar(s8[:, 6:7], s8[:, 5:6], 127.0, None, ALU.mult))
                    wv(ve)
                    VE(ve.reciprocal(s8[:, 7:8], s8[:, 6:7]))
                    wv(ve)
                    VE(ve.tensor_scalar(fgv[:, t:t + 1], s8[:, 7:8], mgv[:, 0:1], None, ALU.mult))
                    VE(ve.tensor_scalar(z[:], ht[:], s8[:, 6:7], None, ALU.mult))
                    wv(ve)
                    ve.wait_ge(ct, fh_done[t % 2])
                    binary_round(hq[:, (t % 2) * D:(t % 2) * D + D], z[:], sq[:],
                                 r[:], tb[:])
                    wv(tn)
                    for k in range(NT):
                        wsc(tn)
                        TN(tn.transpose(PSB[:, 0:128],
                                        hq[:, (t % 2) * D + k * 128:
                                            (t % 2) * D + (k + 1) * 128], idb[:]))
                        wt(sc)
                        SC(sc.activation(hqT[:, k * 2048 + t * 128: k * 2048 + t * 128 + 128],
                                         PSB[:, 0:128], AF.Copy))
                    fh_done[t % 2] = cnt["ct"]
                fence()
            with ExitStack() as ph2:
                wraw = ph2.enter_context(nc.sbuf_tensor("fg_w", [128, 4 * 2048], F32))
                wq = ph2.enter_context(nc.sbuf_tensor("fg_q", [128, 4 * 2048], BF16))
                stp = ph2.enter_context(nc.sbuf_tensor("fg_st", [128, 4 * 2048], F32))
                wTv = ph2.enter_context(nc.sbuf_tensor("fg_tv", [128, NT * 512], BF16))
                wTg = ph2.enter_context(nc.sbuf_tensor("fg_tg", [128, NT * 512], BF16))
                sg = ph2.enter_context(nc.sbuf_tensor("fg_s", [128, 512], F32))
                ut = ph2.enter_context(nc.sbuf_tensor("fg_u", [128, 512], F32))
                rm = ph2.enter_context(nc.sbuf_tensor("fg_rm", [128, 1], F32))

                def quant_w_to(dstT, og, thr):
                    wv(sy); wsc(sy); wt(sy)
                    nlo = dmas(wraw[:, 0:4096].rearrange("p (a c) -> p a c", a=2),
                               Wd["wgv"][og * 512: og * 512 + 256, :]
                               .rearrange("(a p) c -> p a c", p=128))
                    nhi = dmas(wraw[:, 4096:8192].rearrange("p (a c) -> p a c", a=2),
                               Wd["wgv"][og * 512 + 256:(og + 1) * 512, :]
                               .rearrange("(a p) c -> p a c", p=128))
                    for hf_ in range(2):
                        n_ = nlo if hf_ == 0 else nhi
                        sl = slice(hf_ * 4096, (hf_ + 1) * 4096)
                        ve.wait_ge(ds, n_); sc.wait_ge(ds, n_)
                        tn.wait_ge(ds, n_); heatmm()
                        wv(sc); wsc(sc)
                        SC(sc.activation(stp[:, sl], wraw[:, sl], AF.Abs))
                        heats()
                        wsc(ve); wv(ve)
                        VE(ve.tensor_scalar(stp[:, sl], stp[:, sl], thr[:, 0:1],
                                            None, ALU.is_ge))
                        wv(ve); heatv()
                        VE(ve.tensor_tensor(wraw[:, sl], wraw[:, sl], stp[:, sl],
                                            ALU.mult))
                        wv(sc); wsc(sc); heatv()
                        SC(sc.activation(wq[:, sl], wraw[:, sl], AF.Sign))
                        heats()
                    wsc(tn); wsc(ve)
                    for a in range(4):
                        for k in range(NT):
                            wv(tn)
                            TN(tn.transpose(PSB[:, 0:128],
                                            wq[:, a * 2048 + k * 128: a * 2048 + k * 128 + 128],
                                            idb[:]))
                            wt(ve)
                            VE(ve.tensor_copy(
                                dstT[:, k * 512 + a * 128: k * 512 + a * 128 + 128],
                                PSB[:, 0:128]))

                for og in range(16):
                    quant_w_to(wTv, og, thrgv)
                    quant_w_to(wTg, og + 16, thrgv)
                    for t in range(NT):
                        pv = PS[t % 2]
                        pg = PS[2 + t % 2]
                        tn.wait_ge(cv, cnt["cv"])
                        for k in range(NT):
                            (TN if k == NT - 1 else TNq)(
                                tn.matmul(pv[:, 0:512],
                                          hqT[:, k * 2048 + t * 128: k * 2048 + t * 128 + 128],
                                          wTv[:, k * 512:(k + 1) * 512],
                                          start=(k == 0), stop=(k == NT - 1)))
                        for k in range(NT):
                            (TN if k == NT - 1 else TNq)(
                                tn.matmul(pg[:, 0:512],
                                          hqT[:, k * 2048 + t * 128: k * 2048 + t * 128 + 128],
                                          wTg[:, k * 512:(k + 1) * 512],
                                          start=(k == 0), stop=(k == NT - 1)))
                        ve.wait_ge(ct, cnt["ct"])
                        sc.wait_ge(ct, cnt["ct"])
                        ve.wait_ge(dg, cnt["dg"])
                        sc.wait_ge(dg, cnt["dg"])
                        VE(ve.tensor_scalar(sg[:], pg[:, 0:512], fgv[:, t:t + 1], None, ALU.mult))
                        wv(sc); wsc(sc)
                        SC(sc.activation(sg[:], sg[:], AF.Silu))
                        VE(ve.tensor_scalar(ut[:], pv[:, 0:512], fgv[:, t:t + 1], None, ALU.mult))
                        wsc(ve); wv(ve)
                        VE(ve.tensor_tensor(ut[:], ut[:], sg[:], ALU.mult))
                        wv(ve); heatv()
                        VE(ve.reduce_max(rm[:], ut[:], axis=AX, apply_absolute_value=True))
                        wv(ve)
                        VE(ve.tensor_tensor(umax[:, t:t + 1], umax[:, t:t + 1], rm[:], ALU.max))
                        wv(gp)
                        dmag(s_u[t * 128:(t + 1) * 128, og * 512:(og + 1) * 512], ut[:])
                fence()

        _ckpt(6)
        with ExitStack() as ph:
            wraw = ph.enter_context(nc.sbuf_tensor("fp_w", [128, 8192], F32))
            stp = ph.enter_context(nc.sbuf_tensor("fp_st", [128, 8192], F32))
            wq = ph.enter_context(nc.sbuf_tensor("fp_q", [128, 8192], BF16))
            col = ph.enter_context(nc.sbuf_tensor("fp_c", [128, 64 * 128], BF16))
            for g in range(16):
                wv(sy); wsc(sy); wdg(sy)
                nlo = dmas(wraw[:, 0:4096], Wd["wout"][g * 128:(g + 1) * 128, 0:4096])
                nhi = dmas(wraw[:, 4096:8192],
                           Wd["wout"][g * 128:(g + 1) * 128, 4096:8192])
                for hf_ in range(2):
                    n_ = nlo if hf_ == 0 else nhi
                    sl = slice(hf_ * 4096, (hf_ + 1) * 4096)
                    ve.wait_ge(ds, n_); sc.wait_ge(ds, n_)
                    tn.wait_ge(ds, n_); heatmm()
                    wv(sc); wsc(sc)
                    SC(sc.activation(stp[:, sl], wraw[:, sl], AF.Abs))
                    heats()
                    wsc(ve); wv(ve)
                    VE(ve.tensor_scalar(stp[:, sl], stp[:, sl], thrwo[:, 0:1],
                                        None, ALU.is_ge))
                    wv(ve); heatv()
                    VE(ve.tensor_tensor(wraw[:, sl], wraw[:, sl], stp[:, sl],
                                        ALU.mult))
                    wv(sc); wsc(sc); heatv()
                    SC(sc.activation(wq[:, sl], wraw[:, sl], AF.Sign))
                    heats()
                wsc(tn); wsc(ve)
                wdg(ve)
                for k in range(64):
                    wv(tn)
                    TN(tn.transpose(PSB[:, 0:128], wq[:, k * 128:(k + 1) * 128], idb[:]))
                    wt(ve)
                    VE(ve.tensor_copy(col[:, k * 128:(k + 1) * 128], PSB[:, 0:128]))
                wv(gp)
                dmag(s_wo[g // 4, :, :, (g % 4) * 128:(g % 4) * 128 + 128]
                     .rearrange("k p c -> p k c"),
                     col[:].rearrange("p (k c) -> p k c", k=64))
            fence()

        _ckpt(7)
        with ExitStack() as ph:
            ufm = ph.enter_context(nc.sbuf_tensor("fu_t", [128, 64 * 512], BF16))
            utile = ph.enter_context(nc.sbuf_tensor("fu_u", [128, 8192], F32))
            r = ph.enter_context(nc.sbuf_tensor("fu_r", [128, 2048], F32))
            tb = ph.enter_context(nc.sbuf_tensor("fu_b", [128, 2048], F32))
            uq = ph.enter_context(nc.sbuf_tensor("fu_q", [128, 2 * 8192], BF16))
            s8 = ph.enter_context(nc.sbuf_tensor("fu_s", [128, 8], F32))
            x1rw = ph.enter_context(nc.sbuf_tensor("fu_x", [128, 4 * 2048], F32))
            rhs = [ph.enter_context(nc.sbuf_tensor(f"fu_w{i}", [128, 4 * 512], BF16))
                   for i in range(2)]
            tp_done = [0, 0]
            n_u = 0
            for slab in range(4):
                for tl in range(4):
                    t = slab * 4 + tl
                    half = t % 2
                    if t == 0:
                        wv(sy)
                        n_u = dmas(utile[:], s_u[0:128, :])
                    ve.wait_ge(ds, n_u)
                    wv(ve)
                    VE(ve.tensor_scalar(s8[:, 0:1], umax[:, t:t + 1], EPS, None, ALU.max))
                    wv(ve)
                    VE(ve.reciprocal(s8[:, 1:2], s8[:, 0:1]))
                    wv(ve)
                    VE(ve.tensor_scalar(s8[:, 2:3], s8[:, 1:2], 127.0, None, ALU.mult))
                    wv(ve)
                    VE(ve.reciprocal(s8[:, 3:4], s8[:, 2:3]))
                    wv(ve)
                    VE(ve.tensor_scalar(fwo[:, t:t + 1], s8[:, 3:4], mwo[:, 0:1], None, ALU.mult))
                    VE(ve.tensor_scalar(utile[:], utile[:], s8[:, 2:3], None, ALU.mult))
                    wv(ve)
                    ve.wait_ge(ct, tp_done[half])
                    for c_ in range(4):
                        binary_round(uq[:, half * 8192 + c_ * 2048:
                                        half * 8192 + (c_ + 1) * 2048],
                                     utile[:, c_ * 2048:(c_ + 1) * 2048],
                                     utile[:, c_ * 2048:(c_ + 1) * 2048], r[:], tb[:])
                    if t < 15:
                        wv(sy)
                        n_u = dmas(utile[:], s_u[(t + 1) * 128:(t + 2) * 128, :])
                    wv(tn)
                    for k in range(64):
                        wsc(tn)
                        TN(tn.transpose(PSB[:, 0:128],
                                        uq[:, half * 8192 + k * 128:
                                            half * 8192 + (k + 1) * 128], idb[:]))
                        wt(sc)
                        SC(sc.activation(ufm[:, k * 512 + tl * 128:
                                             k * 512 + tl * 128 + 128],
                                         PSB[:, 0:128], AF.Copy))
                    tp_done[half] = cnt["ct"]
                wsc(tn)
                wv(sy); wdg(sy)
                n = dmas(x1rw[:].rearrange("p (a c) -> p a c", a=4),
                         s_x1[slab * 512:(slab + 1) * 512, :]
                         .rearrange("(a p) c -> p a c", p=128))
                ve.wait_ge(ds, n)
                for nn in range(4):
                    ev_at = cnt["cv"]
                    mm_after = {}
                    for hk8 in range(16):
                        buf = rhs[hk8 % 2]
                        if hk8 >= 2:
                            sy.wait_ge(ct, mm_after[hk8 - 2])
                        nr = dmas(buf[:].rearrange("p (k c) -> p k c", k=4),
                                  s_wo[nn, hk8 * 4:(hk8 + 1) * 4, :, :]
                                  .rearrange("k p c -> p k c"))
                        tn.wait_ge(ds, nr)
                        if hk8 == 0:
                            tn.wait_ge(cv, ev_at)
                        for k8 in range(4):
                            hk = hk8 * 4 + k8
                            for m in range(4):
                                (TN if k8 == 3 and m == 3 else TNq)(
                                    tn.matmul(PS[m][:, 0:512],
                                              ufm[:, hk * 512 + m * 128: hk * 512 + m * 128 + 128],
                                              buf[:, k8 * 512:(k8 + 1) * 512],
                                              start=(hk == 0), stop=(hk == 63)))
                        mm_after[hk8] = cnt["ct"]
                    wt(ve)
                    for m in range(4):
                        t = slab * 4 + m
                        wv(ve)
                        VE(ve.tensor_scalar(tb[:, 0:512], PS[m][:, 0:512],
                                            fwo[:, t:t + 1], None, ALU.mult))
                        wv(ve)
                        VE(ve.tensor_add(x1rw[:, m * 2048 + nn * 512: m * 2048 + nn * 512 + 512],
                                         x1rw[:, m * 2048 + nn * 512: m * 2048 + nn * 512 + 512],
                                         tb[:, 0:512]))
                wv(gp); wdg(gp)
                dmag(s_x1[slab * 512:(slab + 1) * 512, :]
                     .rearrange("(a p) c -> p a c", p=128),
                     x1rw[:].rearrange("p (a c) -> p a c", a=4))
                gp.wait_ge(dg, cnt["dg"])
            fence()

        _ckpt(8)
        with ExitStack() as ph:
            xt = ph.enter_context(nc.sbuf_tensor("s1x", [128, D], F32))
            sq = ph.enter_context(nc.sbuf_tensor("s1q", [128, D], F32))
            scr = ph.enter_context(nc.sbuf_tensor("s1s", [128, 8], F32))
            for t in range(NT):
                wv(sy); wsc(sy)
                n = dmas(xt[:], s_x1[t * 128:(t + 1) * 128, :])
                ve.wait_ge(ds, n); sc.wait_ge(ds, n)
                ln_stats(xt[:], mu1[:, t:t + 1], rs1[:, t:t + 1], scr, sq[:])
            fence()

        _ckpt(9)
        with ExitStack() as ph:
            kvtm = ph.enter_context(nc.sbuf_tensor("la_kv", [128, NT * 512], F32))
            kfm = ph.enter_context(nc.sbuf_tensor("la_kf", [128, 2 * 2048], F32))
            fm = ph.enter_context(nc.sbuf_tensor("la_fm", [128, NT * 512], F32))
            wsl = ph.enter_context(nc.sbuf_tensor("la_w", [128, NT * 512], F32))
            xtile = ph.enter_context(nc.sbuf_tensor("la_x", [128, D], F32))
            qfm = ph.enter_context(nc.sbuf_tensor("la_qf", [128, NT * 512], F32))
            sct = ph.enter_context(nc.sbuf_tensor("la_sc", [128, 2048], F32))
            ptT = ph.enter_context(nc.sbuf_tensor("la_pt", [128, NT * 128], F32))
            otile = ph.enter_context(nc.sbuf_tensor("la_o", [128, 2048], F32))
            ntile = otile
            scr = sct
            s8 = ph.enter_context(nc.sbuf_tensor("la_8", [128, 8], F32))
            bkvB = ph.enter_context(nc.sbuf_tensor("la_bk", [128, 512], F32))
            dmag(bkvB[:, 0:256], Wd["l1_bk"][0:1, :].partition_broadcast(128))
            dmag(bkvB[:, 256:512], Wd["l1_bv"][0:1, :].partition_broadcast(128))
            wdg(ve)

            nw = dmas(wsl[:].rearrange("p (n x c) -> p n x c", n=16, x=2)[:, :, 0, :],
                      Wd["l1_wk"].rearrange("(n p) c -> p n c", p=128))
            nw = dmas(wsl[:].rearrange("p (n x c) -> p n x c", n=16, x=2)[:, :, 1, :],
                      Wd["l1_wv"].rearrange("(n p) c -> p n c", p=128))
            tn.wait_ge(ds, nw)
            for slab in range(4):
                wt(ve)
                build_fm(slab, fm, xtile, ntile, scr, mu1, rs1, s_x1)
                for tl in range(4):
                    t = slab * 4 + tl
                    bank = tl % 2
                    tn.wait_ge(cv, cnt["cv"])
                    for k in range(NT):
                        (TN if k == NT - 1 else TNq)(
                            tn.matmul(PS[bank][:, 0:512],
                                      fm[:, k * 512 + tl * 128: k * 512 + tl * 128 + 128],
                                      wsl[:, k * 512:(k + 1) * 512],
                                      start=(k == 0), stop=(k == NT - 1)))
                    wt(ve)
                    VE(ve.tensor_add(kvtm[:, t * 512:(t + 1) * 512], PS[bank][:, 0:512],
                                     bkvB[:]))
            for t in range(NT):
                for kvh in range(2):
                    wv(tn)
                    TN(tn.transpose(PS[5][:, 0:128],
                                    kvtm[:, t * 512 + kvh * 128: t * 512 + kvh * 128 + 128],
                                    idt[:]))
                    wt(ve)
                    VE(ve.tensor_copy(kfm[:, kvh * 2048 + t * 128: kvh * 2048 + t * 128 + 128],
                                      PS[5][:, 0:128]))
            for slab in range(2):
                wt(ve)
                build_fm(slab, fm, xtile, ntile, scr, mu1, rs1, s_x1)
                for qg in range(4):
                    wt(sy)
                    nws = dmas_split(wsl, Wd["l1_wq"][:, qg * 512:(qg + 1) * 512],
                                     2048, 512)
                    tn_wait_split(nws)
                    tn.wait_ge(cv, cnt["cv"])
                    for m in range(4):
                        for k in range(NT):
                            (TN if k == NT - 1 else TNq)(
                                tn.matmul(PS[m][:, 0:512],
                                          wsl[:, k * 512 + m * 128: k * 512 + m * 128 + 128],
                                          fm[:, k * 512:(k + 1) * 512],
                                          start=(k == 0), stop=(k == NT - 1)))
                    wt(ve)
                    for m in range(4):
                        h = qg * 4 + m
                        VE(ve.tensor_scalar(qfm[:, h * 512:(h + 1) * 512], PS[m][:, 0:512],
                                            bqc[:, h:h + 1], None, ALU.add))
                for tl in range(4):
                    for h in range(16):
                        kv = h // 8
                        for sch in range(4):
                            bank = 4 + sch % 2
                            tn.wait_ge(cv, cnt["cv"])
                            wv(tn)
                            TN(tn.matmul(PS[bank][:, 0:512],
                                         qfm[:, h * 512 + tl * 128: h * 512 + tl * 128 + 128],
                                         kfm[:, kv * 2048 + sch * 512: kv * 2048 + sch * 512 + 512],
                                         start=True, stop=True))
                            wt(ve)
                            VE(ve.tensor_scalar(sct[:, sch * 512:(sch + 1) * 512],
                                                PS[bank][:, 0:512], SCALE1, -100.0,
                                                ALU.mult, ALU.max))
                        wv(ve)
                        VE(ve.tensor_scalar(sct[:], sct[:], 100.0, None, ALU.min))
                        wv(ve); heatv()
                        VE(ve.reduce_max(s8[:, 0:1], sct[:], axis=AX))
                        wv(ve)
                        VE(ve.tensor_scalar(s8[:, 1:2], s8[:, 0:1], -1.0, None, ALU.mult))
                        wv(sc); wsc(sc); heatv()
                        SC(sc.activation(sct[:], sct[:], AF.Exp, bias=s8[:, 1:2],
                                         accum_out=s8[:, 2:3]))
                        heats()
                        wsc(ve)
                        VE(ve.reciprocal(s8[:, 3:4], s8[:, 2:3]))
                        wv(ve)
                        VE(ve.tensor_scalar(sct[:], sct[:], s8[:, 3:4], None, ALU.mult))
                        wv(ve); heatv()
                        for k in range(NT):
                            wv(tn)
                            TN(tn.transpose(PS[5][:, 0:128], sct[:, k * 128:(k + 1) * 128],
                                            idt[:]))
                            wt(ve)
                            VE(ve.tensor_copy(ptT[:, k * 128:(k + 1) * 128], PS[5][:, 0:128]))
                        wv(tn); tn.wait_ge(cv, cnt["cv"])
                        for k in range(NT):
                            (TN if k == NT - 1 else TNq)(
                                tn.matmul(PS[6][:, 0:128], ptT[:, k * 128:(k + 1) * 128],
                                          kvtm[:, k * 512 + 256 + kv * 128:
                                               k * 512 + 256 + kv * 128 + 128],
                                          start=(k == 0), stop=(k == NT - 1)))
                        wt(ve)
                        ve.wait_ge(dg, cnt["dg"])
                        VE(ve.tensor_copy(otile[:, h * 128:(h + 1) * 128], PS[6][:, 0:128]))
                    wv(gp)
                    dmag(s_att[(slab * 4 + tl) * 128:(slab * 4 + tl + 1) * 128, :], otile[:])
            fence()

        _ckpt(10)
        with ExitStack() as ph:
            ofm = ph.enter_context(nc.sbuf_tensor("w1_f", [128, NT * 512], F32))
            atile = ph.enter_context(nc.sbuf_tensor("w1_a", [128, 2048], F32))
            wslab = ph.enter_context(nc.sbuf_tensor("w1_w", [128, NT * 512], F32))
            x1rw = ph.enter_context(nc.sbuf_tensor("w1_x", [128, 4 * 2048], F32))
            tmp = ph.enter_context(nc.sbuf_tensor("w1_t", [128, 512], F32))
            boB = ph.enter_context(nc.sbuf_tensor("w1_bo", [128, 2048], F32))
            dmag(boB[:], Wd["l1_bo"][0:1, :].partition_broadcast(128))
            wdg(ve)
            for slab in range(2):
                for tl in range(4):
                    t = slab * 4 + tl
                    wt(sy)
                    n = dmas(atile[:], s_att[t * 128:(t + 1) * 128, :])
                    tn.wait_ge(ds, n)
                    for k in range(NT):
                        wv(tn)
                        TN(tn.transpose(PS[5][:, 0:128], atile[:, k * 128:(k + 1) * 128], idt[:]))
                        wt(ve)
                        VE(ve.tensor_copy(ofm[:, k * 512 + tl * 128: k * 512 + tl * 128 + 128],
                                          PS[5][:, 0:128]))
                wv(sy); wdg(sy)
                n = dmas(x1rw[:].rearrange("p (a c) -> p a c", a=4),
                         s_x1[slab * 512:(slab + 1) * 512, :]
                         .rearrange("(a p) c -> p a c", p=128))
                ve.wait_ge(ds, n)
                for nn in range(4):
                    wt(sy)
                    nws = dmas_split(wslab, Wd["l1_wo"][:, nn * 512:(nn + 1) * 512],
                                     2048, 512)
                    tn_wait_split(nws)
                    for m in range(4):
                        bank = m % 4
                        tn.wait_ge(cv, cnt["cv"])
                        wv(tn)
                        for k in range(NT):
                            (TN if k == NT - 1 else TNq)(
                                tn.matmul(PS[bank][:, 0:512],
                                          ofm[:, k * 512 + m * 128: k * 512 + m * 128 + 128],
                                          wslab[:, k * 512:(k + 1) * 512],
                                          start=(k == 0), stop=(k == NT - 1)))
                        wt(ve); wv(ve)
                        VE(ve.tensor_add(tmp[:], PS[bank][:, 0:512],
                                         boB[:, nn * 512:(nn + 1) * 512]))
                        wv(ve)
                        VE(ve.tensor_add(x1rw[:, m * 2048 + nn * 512: m * 2048 + nn * 512 + 512],
                                         x1rw[:, m * 2048 + nn * 512: m * 2048 + nn * 512 + 512],
                                         tmp[:]))
                wv(gp); wdg(gp)
                dmag(s_y[slab * 512:(slab + 1) * 512, :]
                     .rearrange("(a p) c -> p a c", p=128),
                     x1rw[:].rearrange("p (a c) -> p a c", a=4))
                gp.wait_ge(dg, cnt["dg"])
            fence()

        _ckpt(11)
        with ExitStack() as ph:
            fm = ph.enter_context(nc.sbuf_tensor("mo_fm", [128, NT * 512], F32))
            acc = ph.enter_context(nc.sbuf_tensor("mo_ac", [128, 4 * 2048], F32))
            gt = ph.enter_context(nc.sbuf_tensor("mo_g", [128, 8], F32))
            g8 = ph.enter_context(nc.sbuf_tensor("mo_8", [128, 24], F32))
            wgsb = ph.enter_context(nc.sbuf_tensor("mo_wg", [128, NT * 8], F32))
            beB = ph.enter_context(nc.sbuf_tensor("mo_be", [128, 2048], F32))
            bdB = ph.enter_context(nc.sbuf_tensor("mo_bd", [128, 2048], F32))
            for g in range(2):
                with ExitStack() as phx:
                    xtile = phx.enter_context(nc.sbuf_tensor(f"mo_x{g}", [128, D], F32))
                    ntile = phx.enter_context(nc.sbuf_tensor(f"mo_n{g}", [128, D], F32))
                    scr = phx.enter_context(nc.sbuf_tensor(f"mo_s{g}", [128, D], F32))
                    wt(ve)
                    build_fm(g, fm, xtile, ntile, scr, mu1, rs1, s_x1)
                    fence()
                wt(sy)
                nw = dmas(wgsb[:].rearrange("p (n c) -> p n c", n=16), Wd["wg"].rearrange("(n p) c -> p n c", p=128))
                tn.wait_ge(ds, nw)
                for m in range(4):
                    tn.wait_ge(cv, cnt["cv"])
                    wv(tn)
                    for k in range(NT):
                        (TN if k == NT - 1 else TNq)(
                            tn.matmul(PS[6][:, 0:8],
                                      fm[:, k * 512 + m * 128: k * 512 + m * 128 + 128],
                                      wgsb[:, k * 8:(k + 1) * 8],
                                      start=(k == 0), stop=(k == NT - 1)))
                    wt(ve); wv(ve)
                    VE(ve.tensor_add(gt[:], PS[6][:, 0:8], bgB[:]))
                    wv(ve)
                    VE(ve.tensor_reduce(g8[:, 0:1], gt[:], AX, ALU.min))
                    wv(ve)
                    VE(ve.tensor_scalar(g8[:, 8:16], gt[:], g8[:, 0:1], None, ALU.is_equal))
                    wv(ve)
                    VE(ve.tensor_tensor_scan(g8[:, 16:24], onest[:, 0:8], g8[:, 8:16],
                                             0.0, ALU.mult, ALU.add))
                    wv(ve)
                    VE(ve.reduce_sum(g8[:, 1:2], g8[:, 8:16], axis=AX))
                    wv(ve)
                    VE(ve.tensor_scalar(g8[:, 16:24], g8[:, 16:24], g8[:, 1:2], None,
                                        ALU.is_equal))
                    wv(ve)
                    VE(ve.tensor_tensor(g8[:, 8:16], g8[:, 8:16], g8[:, 16:24], ALU.mult))
                    wv(ve)
                    VE(ve.scalar_tensor_tensor(g8[:, 16:24], g8[:, 8:16], 1e30, gt[:],
                                               ALU.mult, ALU.add))
                    wv(ve)
                    VE(ve.tensor_reduce(g8[:, 2:3], g8[:, 16:24], AX, ALU.min))
                    wv(ve)
                    VE(ve.tensor_scalar(g8[:, 16:24], g8[:, 16:24], g8[:, 2:3], None,
                                        ALU.is_equal))
                    wv(ve)
                    VE(ve.tensor_tensor_scan(g8[:, 0:8], onest[:, 0:8], g8[:, 16:24],
                                             0.0, ALU.mult, ALU.add))
                    wv(ve)
                    VE(ve.reduce_sum(g8[:, 3:4], g8[:, 16:24], axis=AX))
                    wv(ve)
                    VE(ve.tensor_scalar(g8[:, 0:8], g8[:, 0:8], g8[:, 3:4], None,
                                        ALU.is_equal))
                    wv(ve)
                    VE(ve.tensor_tensor(g8[:, 16:24], g8[:, 16:24], g8[:, 0:8], ALU.mult))
                    wv(ve)
                    VE(ve.tensor_add(g8[:, 8:16], g8[:, 8:16], g8[:, 16:24]))
                    wv(ve)
                    VE(ve.scalar_tensor_tensor(g8[:, 16:24], g8[:, 8:16], -1e30, gt[:],
                                               ALU.mult, ALU.add))
                    wv(ve)
                    VE(ve.reduce_max(g8[:, 4:5], g8[:, 16:24], axis=AX))
                    wv(ve)
                    VE(ve.tensor_scalar(g8[:, 5:6], g8[:, 4:5], -1.0, None, ALU.mult))
                    wv(sc); wsc(sc)
                    SC(sc.activation(g8[:, 16:24], g8[:, 16:24], AF.Exp, bias=g8[:, 5:6],
                                     accum_out=g8[:, 6:7]))
                    wsc(ve)
                    VE(ve.reciprocal(g8[:, 7:8], g8[:, 6:7]))
                    wv(ve)
                    VE(ve.tensor_scalar(g8[:, 16:24], g8[:, 16:24], g8[:, 7:8], None, ALU.mult))
                    wv(ve)
                    VE(ve.tensor_scalar(g8[:, 8:16], g8[:, 8:16], -1.0, 1.0, ALU.mult, ALU.add))
                    wv(ve)
                    VE(ve.tensor_tensor(wsel[:, (g * 4 + m) * 8:(g * 4 + m) * 8 + 8],
                                        g8[:, 16:24], g8[:, 8:16], ALU.mult))
                    wv(ve)
                wv(sy); wdg(sy)
                n = dmas(acc[:].rearrange("p (a c) -> p a c", a=4),
                         s_y[g * 512:(g + 1) * 512, :]
                         .rearrange("(a p) c -> p a c", p=128))
                ve.wait_ge(ds, n)
                with ExitStack() as ph2:
                    wge = ph2.enter_context(nc.sbuf_tensor(f"mo_we{g}", [128, NT * 512], F32))
                    wue = ph2.enter_context(nc.sbuf_tensor(f"mo_wu{g}", [128, NT * 512], F32))
                    he = ph2.enter_context(nc.sbuf_tensor(f"mo_he{g}", [128, 4 * 1024], F32))
                    heT = ph2.enter_context(nc.sbuf_tensor(f"mo_ht{g}", [128, 8 * 512], F32))
                    tmp = ph2.enter_context(nc.sbuf_tensor(f"mo_t{g}", [128, 512], F32))
                    for e in range(6):
                        wv(gp)
                        dmag(beB[:, 0:1024], Wd["bgate"][e:e + 1, :].partition_broadcast(128))
                        dmag(beB[:, 1024:2048], Wd["bup"][e:e + 1, :].partition_broadcast(128))
                        dmag(bdB[:], Wd["bdown"][e:e + 1, :].partition_broadcast(128))
                        for nch in range(2):
                            wt(sy)
                            ngs = dmas_split(wge,
                                             Wd["wgate"][e, :, nch * 512:(nch + 1) * 512],
                                             2048, 512)
                            nw = dmas(wue[:].rearrange("p (n c) -> p n c", n=16),
                                      Wd["wup"][e, :, nch * 512:(nch + 1) * 512]
                                      .rearrange("(n p) c -> p n c", p=128))
                            tn_wait_split(ngs)
                            tn.wait_ge(cv, cnt["cv"])
                            for m in range(4):
                                for k in range(NT):
                                    (TN if k == NT - 1 else TNq)(
                                        tn.matmul(PS[m][:, 0:512],
                                                  fm[:, k * 512 + m * 128: k * 512 + m * 128 + 128],
                                                  wge[:, k * 512:(k + 1) * 512],
                                                  start=(k == 0), stop=(k == NT - 1)))
                            wt(ve); wt(sc)
                            ve.wait_ge(dg, cnt["dg"])
                            sc.wait_ge(dg, cnt["dg"])
                            for m in range(4):
                                wv(ve)
                                VE(ve.tensor_add(tmp[:], PS[m][:, 0:512],
                                                 beB[:, nch * 512: nch * 512 + 512]))
                                wv(sc); wsc(sc)
                                SC(sc.activation(he[:, m * 1024 + nch * 512:
                                                    m * 1024 + nch * 512 + 512],
                                                 tmp[:], AF.Silu))
                            wsc(tn); wv(tn)
                            tn.wait_ge(ds, nw)
                            for m in range(4):
                                for k in range(NT):
                                    (TN if k == NT - 1 else TNq)(
                                        tn.matmul(PS[m][:, 0:512],
                                                  fm[:, k * 512 + m * 128: k * 512 + m * 128 + 128],
                                                  wue[:, k * 512:(k + 1) * 512],
                                                  start=(k == 0), stop=(k == NT - 1)))
                            wt(ve); wt(sc); wsc(ve)
                            for m in range(4):
                                wv(ve)
                                VE(ve.tensor_add(tmp[:], PS[m][:, 0:512],
                                                 beB[:, 1024 + nch * 512: 1024 + nch * 512 + 512]))
                                wv(ve)
                                VE(ve.tensor_tensor(he[:, m * 1024 + nch * 512:
                                                       m * 1024 + nch * 512 + 512],
                                                    he[:, m * 1024 + nch * 512:
                                                       m * 1024 + nch * 512 + 512],
                                                    tmp[:], ALU.mult))
                        for m in range(4):
                            for k2 in range(8):
                                wv(tn)
                                TN(tn.transpose(PS[5][:, 0:128],
                                                he[:, m * 1024 + k2 * 128: m * 1024 + k2 * 128 + 128],
                                                idt[:]))
                                wt(ve)
                                VE(ve.tensor_copy(heT[:, k2 * 512 + m * 128: k2 * 512 + m * 128 + 128],
                                                  PS[5][:, 0:128]))
                        dn_after = [cnt["ct"], cnt["ct"]]

                        def dload(nn_):
                            sy.wait_ge(ct, dn_after[nn_ % 2])
                            off = (nn_ % 2) * 4096
                            return dmas(wue[:, off: off + 4096]
                                        .rearrange("p (n c) -> p n c", n=8),
                                        Wd["wdown"][e, :, nn_ * 512:(nn_ + 1) * 512]
                                        .rearrange("(n p) c -> p n c", p=128))
                        nw_d = dload(0)
                        for nn in range(4):
                            half = nn % 2
                            nw_cur = nw_d
                            if nn < 3:
                                nw_d = dload(nn + 1)
                            tn.wait_ge(ds, nw_cur)
                            for m in range(4):
                                bank = m % 4
                                tn.wait_ge(cv, cnt["cv"])
                                wv(tn)
                                for k2 in range(8):
                                    (TN if k2 == 7 else TNq)(
                                        tn.matmul(PS[bank][:, 0:512],
                                                  heT[:, k2 * 512 + m * 128: k2 * 512 + m * 128 + 128],
                                                  wue[:, half * 4096 + k2 * 512:
                                                      half * 4096 + (k2 + 1) * 512],
                                                  start=(k2 == 0), stop=(k2 == 7)))
                                wt(ve); wdg(ve); wv(ve); wsc(ve)
                                VE(ve.tensor_add(tmp[:], PS[bank][:, 0:512],
                                                 bdB[:, nn * 512:(nn + 1) * 512]))
                                wv(ve)
                                VE(ve.scalar_tensor_tensor(
                                    acc[:, m * 2048 + nn * 512: m * 2048 + nn * 512 + 512],
                                    tmp[:], wsel[:, (g * 4 + m) * 8 + e:(g * 4 + m) * 8 + e + 1],
                                    acc[:, m * 2048 + nn * 512: m * 2048 + nn * 512 + 512],
                                    ALU.mult, ALU.add))
                            dn_after[half] = cnt["ct"]
                with ExitStack() as ph3:
                    wsu = ph3.enter_context(nc.sbuf_tensor(f"mo_su{g}", [128, NT * 512], F32))
                    sh = ph3.enter_context(nc.sbuf_tensor(f"mo_sh{g}", [128, 4 * 2048], F32))
                    shT = ph3.enter_context(nc.sbuf_tensor(f"mo_st{g}", [128, NT * 512], F32))
                    tmp = ph3.enter_context(nc.sbuf_tensor(f"mo_t2{g}", [128, 512], F32))
                    wv(gp)
                    dmag(beB[:], Wd["bs_up"][0:1, :].partition_broadcast(128))
                    dmag(bdB[:], Wd["bs_down"][0:1, :].partition_broadcast(128))
                    for nch in range(4):
                        wt(sy)
                        nws = dmas_split(wsu, Wd["ws_up"][:, nch * 512:(nch + 1) * 512],
                                         2048, 512)
                        tn_wait_split(nws)
                        for m in range(4):
                            bank = m % 4
                            tn.wait_ge(cv, cnt["cv"])
                            wv(tn)
                            for k in range(NT):
                                (TN if k == NT - 1 else TNq)(
                                    tn.matmul(PS[bank][:, 0:512],
                                              fm[:, k * 512 + m * 128: k * 512 + m * 128 + 128],
                                              wsu[:, k * 512:(k + 1) * 512],
                                              start=(k == 0), stop=(k == NT - 1)))
                            wt(ve); wdg(ve); wv(ve); wsc(ve)
                            VE(ve.tensor_add(tmp[:], PS[bank][:, 0:512],
                                             beB[:, nch * 512:(nch + 1) * 512]))
                            wv(sc); wsc(sc)
                            SC(sc.activation(sh[:, m * 2048 + nch * 512:
                                                m * 2048 + nch * 512 + 512],
                                             tmp[:], AF.Silu))
                    wsc(tn); wsc(ve)
                    for m in range(4):
                        for k in range(NT):
                            wv(tn)
                            TN(tn.transpose(PS[5][:, 0:128],
                                            sh[:, m * 2048 + k * 128: m * 2048 + k * 128 + 128],
                                            idt[:]))
                            wt(ve)
                            VE(ve.tensor_copy(shT[:, k * 512 + m * 128: k * 512 + m * 128 + 128],
                                              PS[5][:, 0:128]))
                    for nn in range(4):
                        wt(sy)
                        nws = dmas_split(wsu, Wd["ws_down"][:, nn * 512:(nn + 1) * 512],
                                         2048, 512)
                        tn_wait_split(nws)
                        for m in range(4):
                            bank = m % 4
                            tn.wait_ge(cv, cnt["cv"])
                            wv(tn)
                            for k in range(NT):
                                (TN if k == NT - 1 else TNq)(
                                    tn.matmul(PS[bank][:, 0:512],
                                              shT[:, k * 512 + m * 128: k * 512 + m * 128 + 128],
                                              wsu[:, k * 512:(k + 1) * 512],
                                              start=(k == 0), stop=(k == NT - 1)))
                            wt(ve); wdg(ve); wv(ve); wsc(ve)
                            VE(ve.tensor_add(tmp[:], PS[bank][:, 0:512],
                                             bdB[:, nn * 512:(nn + 1) * 512]))
                            wv(ve)
                            VE(ve.tensor_add(acc[:, m * 2048 + nn * 512:
                                                 m * 2048 + nn * 512 + 512],
                                             acc[:, m * 2048 + nn * 512:
                                                 m * 2048 + nn * 512 + 512],
                                             tmp[:]))
                wv(gp); wdg(gp)
                dmag(out_d[g * 512:(g + 1) * 512, :]
                     .rearrange("(a p) c -> p a c", p=128),
                     acc[:].rearrange("p (a c) -> p a c", a=4))
            wdg(gp); wds(gp)
    except _StopBuild:
        for e in (tn, ve, sc, gp, sy):
            wds(e); wdg(e); wt(e); wv(e); wsc(e)

    if DEBUG:
        for nm_, sb_, shp_ in [("qg", qgsum, [1, 2048]), ("al", alsb, [128, 512]),
                               ("os", osbd, [128, 2048]), ("fgv", fgv, [128, 16]),
                               ("fwo", fwo, [128, 16]), ("wsel", wsel, [128, 64])]:
            o_ = nc.dram_tensor("dbgs_" + nm_, shp_, F32, kind="ExternalOutput").ap()
            dmag(o_[:, :], sb_[:])
        dbg = {"x1": (s_x1, [T, D]), "qk": (s_qk, [T, 2048]),
               "kk": (s_kk, [T, 512]), "v": (s_v, [T, 512]),
               "u": (s_u, [T, 8192]), "att": (s_att, [OWN, 2048]),
               "y": (s_y, [OWN, D])}
        for e in (tn, ve, sc, gp, sy):
            wds(e); wdg(e); wt(e); wv(e); wsc(e)
        for name, (src_t, shp) in dbg.items():
            o = nc.dram_tensor("dbg_" + name, shp, F32, kind="ExternalOutput").ap()
            dmag(o[:, :], src_t[:, :])
        wdg(gp)

    def _replay(rec):
        def emit(e):
            for name, a, k, fi in rec.ops:
                inst = getattr(e, name)(*a, **k)
                for sem_, n_ in fi.incs:
                    inst.then_inc(sem_, n_)
        return emit

    blk.tensor(_replay(tn))
    blk.vector(_replay(ve))
    blk.scalar(_replay(sc))
    blk.gpsimd(_replay(gp))
    blk.sync(_replay(sy))

    ost.close()
    return nc


def _host_consts():
    inv = 1.0 / (10000.0 ** (np.arange(0, 64, 2, dtype=np.float64) / 64.0))
    pos = np.arange(T, dtype=np.float64)
    fr = pos[:, None] * inv[None, :]
    emb = np.concatenate([fr, fr], axis=1)
    cs = np.concatenate([np.cos(emb), np.sin(emb)], axis=1).astype(np.float32)
    ident = np.eye(128, dtype=np.float32)
    import ml_dtypes
    identb = ident.astype(ml_dtypes.bfloat16)
    ones = np.ones((128, 128), dtype=np.float32)
    return cs, ident, identb, ones


_NC_CACHE = [None]


def kernel(**inputs):
    cs, ident, identb, ones = _host_consts()
    if _NC_CACHE[0] is None:
        _NC_CACHE[0] = build_nc()
    nc = _NC_CACHE[0]
    x0 = np.ascontiguousarray(np.asarray(inputs["L0_attn_hidden_states"], np.float32))
    keymap = {
        "l0_wq": "L0_attn_Wq", "l0_wk": "L0_attn_Wk", "l0_wv": "L0_attn_Wv",
        "l0_wphi": "L0_attn_Wphi", "l0_wo": "L0_attn_Wo",
        "wgv": "L0_ffn_w_gv", "wout": "L0_ffn_w_out",
        "l1_wq": "L1_attn_Wq", "l1_wk": "L1_attn_Wk", "l1_wv": "L1_attn_Wv",
        "l1_wo": "L1_attn_Wo",
        "wg": "L1_ffn_Wg", "wgate": "L1_ffn_Wgate", "wup": "L1_ffn_Wup",
        "wdown": "L1_ffn_Wdown", "ws_up": "L1_ffn_Ws_up", "ws_down": "L1_ffn_Ws_down",
    }
    row1 = {
        "wnorm": "L0_ffn_w_norm", "l1_bq": "L1_attn_bq", "l1_bk": "L1_attn_bk",
        "l1_bv": "L1_attn_bv", "l1_bo": "L1_attn_bo", "bg": "L1_ffn_bg",
        "bs_up": "L1_ffn_bs_up", "bs_down": "L1_ffn_bs_down",
    }
    shared = {}
    for k, src in keymap.items():
        shared[k] = np.ascontiguousarray(np.asarray(inputs[src], np.float32))
    for k, src in row1.items():
        shared[k] = np.ascontiguousarray(
            np.asarray(inputs[src], np.float32).reshape(1, -1))
    for k, src in {"bgate": "L1_ffn_bgate", "bup": "L1_ffn_bup",
                   "bdown": "L1_ffn_bdown"}.items():
        shared[k] = np.ascontiguousarray(np.asarray(inputs[src], np.float32))
    shared["c_id"] = ident
    shared["c_idb"] = identb
    shared["c_ones"] = ones

    in_maps = []
    for c in range(8):
        b, h = c // 2, c % 2
        m = dict(shared)
        xb = x0[b]
        if h == 1:
            xb = np.concatenate([xb[1024:], xb[:1024]], axis=0)
            m["c_cs"] = np.ascontiguousarray(
                np.concatenate([cs[1024:], cs[:1024]], axis=0))
        else:
            m["c_cs"] = cs
        m["x0"] = np.ascontiguousarray(xb)
        in_maps.append(m)

    res = run_bass_kernel_spmd(nc, in_maps, core_ids=list(range(8)))
    out = np.empty((4, T, D), dtype=np.float32)
    for c in range(8):
        b, h = c // 2, c % 2
        out[b, h * 1024:(h + 1) * 1024] = res.results[c]["out"]
    return out
